# version 36
# baseline (speedup 1.0000x reference)
import math
import numpy as np
import ml_dtypes
import concourse.bass as bass
import concourse.mybir as mybir
from concourse.bass_utils import run_bass_kernel_spmd

F32 = mybir.dt.float32
BF16 = mybir.dt.bfloat16
AF = mybir.ActivationFunctionType
ALU = mybir.AluOpType
AX = mybir.AxisListType
BF = ml_dtypes.bfloat16

D = 1024
KC = 8
MASKV = -60.0

ENGINES = ("tensor", "vector", "scalar", "sync", "gpsimd")


class Em:
    def __init__(self):
        self.q = {e: [] for e in ENGINES}
        self.cnt = {}
        self.lastwait = {e: {} for e in ENGINES}

    def now(self, sem):
        return self.cnt.get(sem, 0)

    def op(self, eng, fn, waits=(), inc=None, inc_by=1):
        lw = self.lastwait[eng]
        ws = []
        for s, v in waits:
            if v is None or v <= 0:
                continue
            if v > lw.get(s, 0):
                lw[s] = v
                ws.append((s, v))
        if inc is not None:
            self.cnt[inc] = self.now(inc) + inc_by
        self.q[eng].append((tuple(ws), fn, inc, inc_by))
        return self.now(inc) if inc is not None else None


def ceil_div(a, b):
    return -(-a // b)


def build_program(S, NSEQ):
    G = S // 512
    NCH = S // 128
    NTC = S // 128
    nc = bass.Bass("TRN2", target_bir_lowering=False)

    x_d = nc.dram_tensor("x", [NSEQ * S, D], F32, kind="ExternalInput")
    wq_d = nc.dram_tensor("wq", [D, D], F32, kind="ExternalInput")
    wk_d = nc.dram_tensor("wk", [D, D], F32, kind="ExternalInput")
    wv_d = nc.dram_tensor("wv", [D, D], F32, kind="ExternalInput")
    wo_d = nc.dram_tensor("wo", [D, D], F32, kind="ExternalInput")
    wb_d = nc.dram_tensor("wbase", [D, D], F32, kind="ExternalInput")
    acat_d = nc.dram_tensor("acat", [D, 120], F32, kind="ExternalInput")
    bcat_d = nc.dram_tensor("bcat", [128, D], F32, kind="ExternalInput")
    rb_d = nc.dram_tensor("rb", [128, 8], F32, kind="ExternalInput")
    mask_d = nc.dram_tensor("mask", [128, 512], BF16, kind="ExternalInput")
    id_d = nc.dram_tensor("ident", [128, 128], BF16, kind="ExternalInput")
    out_d = nc.dram_tensor("out", [NSEQ * S, D], F32, kind="ExternalOutput")

    em = Em()
    from contextlib import ExitStack
    with ExitStack() as st_:
        def sb(name, shape, dtype):
            return st_.enter_context(nc.sbuf_tensor(name, shape, dtype))

        def psalloc(name, shape, dtype=F32):
            return st_.enter_context(nc.psum_tensor(name, shape, dtype))

        wsb_q = sb("wsb_q", [128, KC * D], BF16)
        wsb_kb = sb("wsb_kb", [128, KC * D], BF16)
        wsb_v = sb("wsb_v", [128, KC * D], BF16)
        wsb_o = sb("wsb_o", [128, KC * D], BF16)
        acat = sb("acat_sb", [128, KC * 120], BF16)
        bcat = sb("bcat_sb", [128, D], BF16)
        xn1 = sb("xn1_tm", [128, NCH * D], BF16)
        kt = sb("kt_fm", [128, KC * S], BF16)
        xnt = sb("xnt_g", [128, KC * 512], BF16)
        qmix = sb("qmix_g", [128, KC * 512], BF16)
        ctx = sb("ctx_g", [128, KC * 512], BF16)
        ptg = sb("pt_g", [128, NTC * 512], BF16)
        pbf = sb("p_bf", [128, S], BF16)
        trash = pbf
        x1b = [sb(f"x1_{i}", [128, D], F32) for i in range(4)]
        xn2 = [sb(f"xn2_{i}", [128, D], BF16) for i in range(4)]
        xn2t = [sb(f"xn2t_{i}", [128, KC * 128], BF16) for i in range(2)]
        hw = [sb(f"hw{i}", [128, 128], BF16) for i in range(2)]
        hwt = [sb(f"hwt{i}", [128, 128], BF16) for i in range(2)]
        mask = sb("mask_sb", [128, 512], BF16)
        ident = sb("ident_sb", [128, 128], BF16)
        rb = sb("rb_sb", [128, 8], F32)
        stats = [sb(f"stats{i}", [128, 8], F32) for i in range(4)]
        pmx = sb("pmx", [128, 4], F32)
        nmx = sb("nmx", [128, 4], F32)
        psum_s = sb("psum_s", [128, 4], F32)
        rsum = sb("rsum", [128, 1], F32)
        recg = sb("recg", [128, 4], F32)
        epst = sb("epst", [128, 1], F32)
        eps6 = sb("eps6", [128, 1], F32)
        lg = [sb(f"lg{i}", [128, 8], F32) for i in range(2)]
        ge6 = [sb(f"ge6_{i}", [128, 8], F32) for i in range(2)]
        se6 = [sb(f"se6_{i}", [128, 1], F32) for i in range(2)]
        top8 = [sb(f"top8_{i}", [128, 8], F32) for i in range(2)]
        den = [sb(f"den{i}", [128, 1], F32) for i in range(2)]
        w6 = [sb(f"w6_{i}", [128, 8], F32) for i in range(2)]
        g_nm = sb("g_nm", [128, 1], F32)
        h_sb = [sb(f"h_sb{i}", [128, 118], F32) for i in range(2)]

        big = psalloc("ps_big", [128, 2048])
        aux = psalloc("ps_aux", [128, 1024])
        psh = psalloc("ps_h", [128, 128])
        tpa = psalloc("ps_tpa", [128, 128], BF16)

        try:
            rem = nc.sbuf_bytes_remaining
            print(f"[kernel build] sbuf bytes remaining/partition: {rem}")
        except Exception:
            pass

        sems = {}
        for s in ("sx", "sw", "so", "sd", "sd2", "st", "sv", "ss", "sg"):
            sems[s] = st_.enter_context(nc.semaphore(s))

        free = {}

        def waitf(key):
            return list(free.get(key, ()))

        def setf(key, *deps):
            free[key] = [d for d in deps if d is not None]

        def addf(key, *deps):
            free.setdefault(key, [])
            free[key].extend(d for d in deps if d is not None)

        def dma(eng, dst, src, waits=(), sem="sx"):
            return em.op(eng, lambda e, d=dst, s=src: e.dma_start(out=d, in_=s),
                         waits=waits, inc=sem, inc_by=16)

        def dmat(dst, src, waits=(), eng="sync"):
            sem = "sd" if eng == "sync" else "sd2"
            return em.op(eng,
                         lambda e, d=dst, s=src: e.dma_start_transpose(d, s),
                         waits=waits, inc=sem, inc_by=16)

        def vop(fn, waits=(), inc="sv"):
            return em.op("vector", fn, waits=waits, inc=inc)

        def sop(fn, waits=(), inc="ss"):
            return em.op("scalar", fn, waits=waits, inc=inc)

        def gop(fn, waits=(), inc="sg"):
            return em.op("gpsimd", fn, waits=waits, inc=inc)

        def top(fn, waits=(), inc="st"):
            return em.op("tensor", fn, waits=waits, inc=inc)

        c_mask = dma("sync", mask[:], mask_d[:])
        c_id = dma("sync", ident[:], id_d[:])
        c_rb = dma("sync", rb[:], rb_d[:])

        def wload(dst, src_d, waits=()):
            return dma("gpsimd", dst.rearrange("p (k n) -> p k n", k=KC),
                       src_d.rearrange("(k p) n -> p k n", p=128), waits=waits,
                       sem="sw")

        def wk_load(waits=()):
            dma("gpsimd", qmix.rearrange("p (k n) -> p k n", k=4),
                wk_d[0:512, :].rearrange("(k p) n -> p k n", p=128),
                waits=list(waits), sem="sw")
            return dma("gpsimd", ctx.rearrange("p (k n) -> p k n", k=4),
                       wk_d[512:1024, :].rearrange("(k p) n -> p k n", p=128),
                       waits=list(waits), sem="sw")

        def wk_sl(k, m):
            buf = qmix if k < 4 else ctx
            return buf[:, (k % 4) * D + m * 128: (k % 4) * D + (m + 1) * 128]

        w_kb = wk_load()
        w_base = wload(wsb_kb, wb_d)
        w_q = wload(wsb_q, wq_d)
        w_v = wload(wsb_v, wv_d)
        w_o = wload(wsb_o, wo_d)
        w_ac = wload(acat, acat_d)
        w_bc = dma("gpsimd", bcat[:], bcat_d[:], sem="sw")

        vop(lambda e: e.memset(epst[:], 1e-5))
        vop(lambda e: e.memset(eps6[:], 1e-6))
        for i in range(2):
            vop(lambda e, i=i: e.memset(ge6[i][:, 6:8], -1e30))
        for i in range(2):
            vop(lambda e, i=i: e.memset(hw[i][:, 112:113], 1.0))
            vop(lambda e, i=i: e.memset(hw[i][:, 113:128], 0.0))

        def bigs(i):
            return big[:, i * 512:(i + 1) * 512]

        def ln_norm(src, dst, par, xwait, dst_wait=(), sum_eng="vector"):
            stt = stats[par]
            sq_done = sop(lambda e, s=src: e.activation(
                out=trash[:, 0:D], in_=s, func=AF.Square, accum_out=stt[:, 1:2]),
                waits=list(xwait) + waitf(("stats", par)))
            if sum_eng == "scalar":
                sum_done = sop(lambda e, s=src: e.activation(
                    out=trash[:, 0:D], in_=s, func=AF.Copy,
                    accum_out=stt[:, 0:1]))
                sdep = ("ss", sum_done)
            else:
                sum_done = vop(lambda e, s=src: e.reduce_sum(
                    stt[:, 0:1], s, axis=AX.X), waits=list(xwait))
                sdep = ("sv", sum_done)
            m_done = vop(lambda e: e.tensor_scalar(
                out=stt[:, 2:3], in0=stt[:, 0:1], scalar1=1.0 / D, scalar2=None,
                op0=ALU.mult), waits=[sdep])
            nv_done = vop(lambda e: e.scalar_tensor_tensor(
                out=stt[:, 4:5], in0=stt[:, 2:3], scalar=stt[:, 0:1],
                in1=stt[:, 1:2], op0=ALU.mult, op1=ALU.subtract),
                waits=[("sv", m_done), ("ss", sq_done)])
            ln_done = sop(lambda e: e.activation(
                out=stt[:, 5:6], in_=stt[:, 4:5], func=AF.Ln, bias=epst[:],
                scale=-1.0 / D), waits=[("sv", nv_done)])
            rstd_done = sop(lambda e: e.activation(
                out=stt[:, 6:7], in_=stt[:, 5:6], func=AF.Exp, scale=-0.5))
            norm_done = vop(lambda e, s=src, d=dst: e.tensor_scalar(
                out=d, in0=s, scalar1=stt[:, 2:3], scalar2=stt[:, 6:7],
                op0=ALU.subtract, op1=ALU.mult),
                waits=[("ss", rstd_done)] + list(dst_wait))
            setf(("stats", par), ("sv", norm_done))
            return ("sv", norm_done)

        def hw_transpose(slot, src_ready):
            t_done = top(lambda e, s=slot: e.transpose(
                tpa[:], hw[s][:], ident[:]),
                waits=list(src_ready) + waitf(("tpa",)) + [("sx", c_id)])
            ev = vop(lambda e, s=slot: e.tensor_copy(hwt[s][:], tpa[:]),
                     waits=[("st", t_done)])
            setf(("tpa",), ("sv", ev))
            return ("sv", ev)

        def mm_group(out_ap, pieces, waits, evac_fn, evac_eng="scalar",
                     psum_key=None):
            n = len(pieces)
            stp = None
            for i, (l, r) in enumerate(pieces):
                w = list(waits) + (waitf(psum_key) if i == 0 else [])
                stp = top(lambda e, l=l, r=r, i=i, o=out_ap: e.matmul(
                    o, l, r, start=(i == 0), stop=(i == n - 1)),
                    waits=w if i == 0 else [], inc="st" if i == n - 1 else None)
            if evac_eng == "scalar":
                ev = sop(lambda e: evac_fn(e), waits=[("st", stp)])
                dep = ("ss", ev)
            else:
                ev = vop(lambda e: evac_fn(e), waits=[("st", stp)])
                dep = ("sv", ev)
            if psum_key is not None:
                setf(psum_key, dep)
            return dep, stp

        nstore = [0]
        base_last = [0]

        def ffn_head_dma(fc):
            sc = fc["sc"]
            ts = sc % 2
            tdp = dmat(
                xn2t[ts].rearrange("p (k s) -> p k s", s=128),
                xn2[sc].rearrange("p (k d) -> p k d", d=128),
                waits=[fc["nd2"]] + waitf(("xn2t", ts)))
            free.pop(("xn2t", ts), None)
            setf(("xn2", sc), ("sd", tdp))
            fc["t2"] = [("sd", tdp)]
            fc["ts"] = ts

        def ffn_head(fc):
            sc, slot = fc["sc"], fc["slot"]
            ts = fc["ts"]
            t2 = fc["t2"]
            h_stp = None
            for k in range(KC):
                w = (t2 + [("sw", w_ac)] + waitf(("psh",))) if k == 0 else []
                h_stp = top(lambda e, k=k, ts=ts: e.matmul(
                    psh[:, 0:120], xn2t[ts][:, k * 128:(k + 1) * 128],
                    acat[:, k * 120:(k + 1) * 120],
                    start=(k == 0), stop=(k == KC - 1)),
                    waits=w, inc="st" if k == KC - 1 else None)
            fc["h_stp"] = h_stp
            hp = sc % 2
            hev = sop(lambda e, hp=hp: e.copy(h_sb[hp][:], psh[:, 0:118]),
                      waits=[("st", h_stp)] + waitf(("h_sb", hp)))
            setf(("psh",), ("ss", hev))
            p = sc % 2
            d1 = vop(lambda e, p=p: e.tensor_tensor(
                lg[p][:, 0:6], h_sb[p][:, 112:118], rb[:, 0:6], ALU.add),
                waits=[("ss", hev), ("sx", c_rb)] + waitf(("lg", p)))
            d3 = sop(lambda e, p=p: e.activation(
                out=ge6[p][:, 0:6], in_=lg[p][:, 0:6], func=AF.Exp,
                accum_out=se6[p][:]), waits=[("sv", d1)])
            setf(("lg", p), ("ss", d3))
            d6 = vop(lambda e, p=p: e.max(out=top8[p][:], in_=ge6[p][:]),
                     waits=[("ss", d3)])
            d7 = vop(lambda e, p=p: e.scalar_tensor_tensor(
                out=ge6[p][:, 0:6], in0=ge6[p][:, 0:6], scalar=top8[p][:, 1:2],
                in1=ge6[p][:, 0:6], op0=ALU.is_ge, op1=ALU.mult),
                waits=[("sv", d6)])
            d8 = vop(lambda e, p=p: e.scalar_tensor_tensor(
                out=den[p][:], in0=se6[p][:], scalar=1e-6, in1=top8[p][:, 0:1],
                op0=ALU.mult, op1=ALU.add))
            d8b = vop(lambda e, p=p: e.tensor_scalar(
                out=den[p][:], in0=den[p][:], scalar1=top8[p][:, 1:2],
                scalar2=None, op0=ALU.add))
            d9 = vop(lambda e, p=p: e.reciprocal(den[p][:], den[p][:]),
                     waits=[("sv", d8b)])
            d10 = vop(lambda e, p=p: e.tensor_scalar(
                out=w6[p][:, 0:6], in0=ge6[p][:, 0:6], scalar1=den[p][:],
                scalar2=None, op0=ALU.mult), waits=[("sv", d9), ("sv", d7)])
            hb_last = vop(lambda e, slot=slot, p=p: e.tensor_copy(
                hw[slot % 2][:, 0:16], h_sb[p][:, 0:16]),
                waits=waitf(("hw", slot % 2)))
            for ex in range(6):
                hb_last = vop(lambda e, ex=ex, slot=slot, p=p: e.tensor_scalar(
                    out=hw[slot % 2][:, 16 + 16 * ex: 32 + 16 * ex],
                    in0=h_sb[p][:, 16 + 16 * ex: 32 + 16 * ex],
                    scalar1=w6[p][:, ex:ex + 1], scalar2=None, op0=ALU.mult),
                    waits=[("sv", d10)] if ex == 0 else [])
            free.pop(("hw", slot % 2), None)
            setf(("h_sb", p), ("sv", hb_last))
            fc["hb_deps"] = [("sv", hb_last)]

        def ffn_tail_base(fc):
            ts = fc["ts"]
            stp = None
            for half in (0, 1):
                for k in range(KC):
                    w = []
                    if k == 0:
                        w = (fc["t2"] + [("sw", fc["w_base"])] +
                             waitf(("aux", half)))
                    last = (half == 1 and k == KC - 1)
                    stp = top(lambda e, k=k, half=half, ts=ts: e.matmul(
                        aux[:, half * 512: half * 512 + 512],
                        xn2t[ts][:, k * 128:(k + 1) * 128],
                        wsb_kb[:, k * D + half * 512: k * D + half * 512 + 512],
                        start=(k == 0), stop=False), waits=w,
                        inc="st" if last else None)
            setf(("xn2t", ts), ("st", stp))

        def ffn_hwt(fc):
            slot = fc["slot"]
            fc["hwt_dep"] = hw_transpose(slot % 2, list(fc["hb_deps"]) +
                                         waitf(("hwt", slot % 2)))
            free.pop(("hwt", slot % 2), None)
            setf(("hw", slot % 2), fc["hwt_dep"])

        def ffn_tail_up(fc):
            seq, g, sc, slot, ts = fc["seq"], fc["g"], fc["sc"], fc["slot"], fc["ts"]
            ci = fc["ci"]
            row0 = fc["row0"]
            ffn_deps = []
            for half in (0, 1):
                stp = top(lambda e, half=half, ts=ts: e.matmul(
                    aux[:, half * 512: half * 512 + 512], hwt[slot % 2][:],
                    bcat[:, half * 512: half * 512 + 512],
                    start=False, stop=True),
                    waits=[fc["hwt_dep"], ("sw", w_bc)], inc="st")
                base_last[0] = stp
                evd = vop(lambda e, half=half, slot=slot: e.tensor_add(
                    x1b[slot][:, half * 512: half * 512 + 512],
                    x1b[slot][:, half * 512: half * 512 + 512],
                    aux[:, half * 512: half * 512 + 512]),
                    waits=[("st", stp), fc["nd2"]])
                ffn_deps.append(("sv", evd))
                setf(("aux", half), ("sv", evd))
            addf(("hwt", slot % 2), *ffn_deps)
            so_c = dma("sync", out_d[row0 + ci * 128: row0 + (ci + 1) * 128, :],
                       x1b[slot][:], waits=ffn_deps, sem="so")
            nstore[0] += 1
            setf(("x1", slot), ("so", so_c))

        boundary_nds = {}
        boundary_xl = {}
        for seq in range(NSEQ):
            row0 = seq * S
            ln1_done = [None] * NCH
            if seq > 0:
                w_kb = wk_load(waits=waitf(("qmix",)) + waitf(("ctx",)))

            kproj_last = 0
            xlv = {}

            def emit_loads(g):
                for c in range(4):
                    ci = g * 4 + c
                    slot = ci % 4
                    xlv[ci] = dma(
                        "sync", x1b[slot][:],
                        x_d[row0 + ci * 128: row0 + (ci + 1) * 128, :],
                        waits=waitf(("x1", slot)))

            def emit_lns(g):
                out = []
                for c in range(4):
                    ci = g * 4 + c
                    slot = ci % 4
                    nd = ln_norm(x1b[slot][:], xn1[:, ci * D:(ci + 1) * D],
                                 ci % 4, [("sx", xlv[ci])],
                                 dst_wait=waitf(("xn1",)))
                    ln1_done[ci] = nd
                    setf(("x1", slot), nd)
                    out.append(nd)
                return out

            if boundary_nds:
                nds = [boundary_nds[c] for c in range(4)]
                for c in range(4):
                    ln1_done[c] = nds[c]
                boundary_nds = {}
                boundary_xl = {}
            else:
                emit_loads(0)
                nds = emit_lns(0)
            for g in range(G):
                if g + 1 < G:
                    emit_loads(g + 1)
                td = []
                for c in range(4):
                    ci = g * 4 + c
                    tdp = dmat(
                        xnt.rearrange("p (k s) -> p k s", s=512)[:, :, c * 128:(c + 1) * 128],
                        xn1.rearrange("p (c k d) -> p c k d", c=NCH, k=KC)[:, ci],
                        waits=[nds[c]] + waitf(("xnt",)))
                    td.append(("sd", tdp))
                if g + 1 < G:
                    nds = emit_lns(g + 1)
                for m in range(KC):
                    sl = m % 4
                    pieces = [(wk_sl(k, m),
                               xnt[:, k * 512:(k + 1) * 512]) for k in range(KC)]
                    dst = kt[:, m * S + g * 512: m * S + (g + 1) * 512]
                    dep, stp = mm_group(
                        bigs(sl), pieces, waits=td + [("sw", w_kb)],
                        evac_fn=(lambda e, d=dst, sl=sl: e.copy(d, bigs(sl))),
                        psum_key=("big", sl))
                    kproj_last = stp
                setf(("xnt",), ("st", kproj_last))
            k_evac_ss = em.now("ss")
            setf(("qmix",), ("st", kproj_last))
            setf(("ctx",), ("st", kproj_last))

            def emit_qtp(g):
                td = []
                for c in range(4):
                    ci = g * 4 + c
                    tdp = dmat(
                        xnt.rearrange("p (k s) -> p k s", s=512)[:, :, c * 128:(c + 1) * 128],
                        xn1.rearrange("p (c k d) -> p c k d", c=NCH, k=KC)[:, ci],
                        waits=[ln1_done[ci]] + waitf(("xnt",)))
                    td.append(("sd", tdp))
                return td

            pending = []
            td_next = emit_qtp(0)
            for g in range(G):
                td = td_next
                q_deps = []
                q_last = 0
                for m in range(KC):
                    sl = m % 4
                    pieces = [(wsb_q[:, k * D + m * 128: k * D + (m + 1) * 128],
                               xnt[:, k * 512:(k + 1) * 512]) for k in range(KC)]
                    dst = qmix[:, m * 512:(m + 1) * 512]
                    dep, stp = mm_group(
                        bigs(sl), pieces,
                        waits=td + [("sw", w_q)] + waitf(("qmix",)),
                        evac_fn=(lambda e, d=dst, sl=sl: e.activation(
                            out=d, in_=bigs(sl), func=AF.Copy, scale=1.0 / 32.0)),
                        psum_key=("big", sl))
                    q_deps.append(dep)
                    q_last = stp
                free.pop(("qmix",), None)
                setf(("xnt",), ("st", q_last))
                if g + 1 < G:
                    td_next = emit_qtp(g + 1)
                pz = vop(lambda e, g=g: e.memset(
                    ptg[:, g * 4 * 512:(g * 4 + 4) * 512], 0.0),
                    waits=waitf(("ptg",)))
                free.pop(("ptg",), None)

                pt_sd = []
                exp_all = 0
                for qt in range(4):
                    qg = g * 4 + qt
                    tmax = (qg + 1) * 128
                    nt = ceil_div(tmax, 512)
                    boff = 2 * (qt % 2) if nt <= 2 else 0
                    stps = []
                    pm_deps = []
                    for tc in range(nt):
                        w = min(512, tmax - tc * 512)
                        stp = None
                        for k in range(KC):
                            ww = []
                            if k == 0:
                                ww = waitf(("big", boff + tc))
                                if tc == 0:
                                    ww = ww + [("ss", k_evac_ss)] + \
                                        [(s, v) for (s, v) in q_deps]
                            stp = top(lambda e, k=k, tc=tc, w=w, qt=qt, boff=boff: e.matmul(
                                big[:, (boff + tc) * 512: (boff + tc) * 512 + w],
                                qmix[:, k * 512 + qt * 128: k * 512 + (qt + 1) * 128],
                                kt[:, k * S + tc * 512: k * S + tc * 512 + w],
                                start=(k == 0), stop=(k == KC - 1)),
                                waits=ww, inc="st" if k == KC - 1 else None)
                        stps.append(stp)
                        if tc == nt - 1 and pending:
                            if qt >= 1:
                                ffn_tail_base(pending[qt - 1])
                                if qt + 1 < 4:
                                    ffn_head_dma(pending[qt + 1])
                        if tc == nt - 1:
                            off = qt * 128
                            mk = vop(lambda e, tc=tc, w=w, off=off, boff=boff: e.tensor_tensor(
                                big[:, (boff + tc) * 512: (boff + tc) * 512 + w],
                                big[:, (boff + tc) * 512: (boff + tc) * 512 + w],
                                mask[:, 384 - off: 384 - off + w], ALU.add),
                                waits=[("st", stp), ("sx", c_mask)])
                            pm = vop(lambda e, tc=tc, w=w, boff=boff: e.reduce_max(
                                pmx[:, tc:tc + 1],
                                big[:, (boff + tc) * 512: (boff + tc) * 512 + w],
                                axis=AX.X), waits=[("sv", mk)])
                        else:
                            pm = vop(lambda e, tc=tc, w=w, boff=boff: e.reduce_max(
                                pmx[:, tc:tc + 1],
                                big[:, (boff + tc) * 512: (boff + tc) * 512 + w],
                                axis=AX.X), waits=[("st", stp)] + waitf(("pmx",)))
                        pm_deps.append(("sv", pm))
                    nm_done = vop(lambda e, nt=nt, qt=qt: e.reduce_max(
                        nmx[:, qt:qt + 1], pmx[:, 0:nt], axis=AX.X, negate=True),
                        waits=pm_deps + waitf(("nmx", qt)))
                    free.pop(("nmx", qt), None)
                    setf(("pmx",), ("sv", nm_done))
                    e_done = 0
                    for tc in range(nt):
                        w = min(512, tmax - tc * 512)
                        e_done = sop(lambda e, tc=tc, w=w, qt=qt, boff=boff: e.activation(
                            out=pbf[:, tc * 512: tc * 512 + w],
                            in_=big[:, (boff + tc) * 512: (boff + tc) * 512 + w],
                            func=AF.Exp, bias=nmx[:, qt:qt + 1],
                            accum_out=psum_s[:, tc:tc + 1]),
                            waits=[("sv", nm_done)] + waitf(("pbf",)) +
                                  waitf(("psum_s",)))
                        setf(("big", boff + tc), ("ss", e_done))
                    free.pop(("pbf",), None)
                    free.pop(("psum_s",), None)
                    exp_all = e_done
                    setf(("nmx", qt), ("ss", e_done))
                    if pending and qt >= 1:
                        ffn_hwt(pending[qt - 1])
                    rs = vop(lambda e, nt=nt: e.reduce_sum(
                        rsum[:], psum_s[:, 0:nt], axis=AX.X),
                        waits=[("ss", e_done)])
                    setf(("psum_s",), ("sv", rs))
                    rc = vop(lambda e, qt=qt: e.reciprocal(
                        recg[:, qt:qt + 1], rsum[:]),
                        waits=[("sv", rs)] + waitf(("recg", qt)))
                    free.pop(("recg", qt), None)
                    setf(("recg2", qt), ("sv", rc))
                    ntc2 = qg + 1
                    pt = dmat(
                        ptg.rearrange("p (c q) -> p c q", q=512)[:, 0:ntc2, qt * 128:(qt + 1) * 128],
                        pbf.rearrange("p (c t) -> p c t", t=128)[:, 0:ntc2, :],
                        waits=[("ss", e_done), ("sv", pz)] + waitf(("ptg_w",)))
                    pt_sd.append(pt)
                    setf(("pbf",), ("sd", pt))
                    if pending:
                        if qt >= 1:
                            ffn_tail_up(pending[qt - 1])
                        ffn_head(pending[qt])
                if pending:
                    ffn_tail_base(pending[3])
                    ffn_hwt(pending[3])
                    ffn_tail_up(pending[3])
                    pending = []

                xlb = {}
                for sc in range(4):
                    ci = g * 4 + sc
                    xlb[sc] = dma("sync", x1b[sc][:],
                                  x_d[row0 + ci * 128: row0 + (ci + 1) * 128, :],
                                  waits=waitf(("x1", sc)))
                ntc2 = (g + 1) * 4
                mix_deps = []
                mix_last = 0
                for dch in range(KC):
                    sl = dch % 4
                    pieces = [(xn1[:, tc2 * D + dch * 128: tc2 * D + (dch + 1) * 128],
                               ptg[:, tc2 * 512:(tc2 + 1) * 512])
                              for tc2 in range(ntc2)]
                    dst = qmix[:, dch * 512:(dch + 1) * 512]
                    dep, stp = mm_group(
                        bigs(sl), pieces,
                        waits=[("sd", pt_sd[-1]), ("ss", exp_all)],
                        evac_fn=(lambda e, d=dst, sl=sl: e.copy(d, bigs(sl))),
                        psum_key=("big", sl))
                    mix_deps.append(dep)
                    mix_last = stp
                setf(("ptg_w",), ("st", mix_last))
                setf(("ptg",), ("st", mix_last))
                ctx_deps = []
                ctx_last = 0
                for dv in range(KC):
                    sl = dv % 4
                    pieces = [(wsb_v[:, k * D + dv * 128: k * D + (dv + 1) * 128],
                               qmix[:, k * 512:(k + 1) * 512]) for k in range(KC)]
                    dst = ctx[:, dv * 512:(dv + 1) * 512]
                    dep, stp = mm_group(
                        bigs(sl), pieces,
                        waits=[(s, v) for (s, v) in mix_deps] + [("sw", w_v)] +
                              waitf(("ctx",)),
                        evac_fn=(lambda e, d=dst, sl=sl: e.copy(d, bigs(sl))),
                        psum_key=("big", sl))
                    ctx_deps.append(dep)
                    ctx_last = stp
                free.pop(("ctx",), None)
                setf(("qmix",), ("st", ctx_last))

                wo_last = 0
                for sc in range(4):
                    ci = g * 4 + sc
                    slot = sc
                    xl = xlb[sc]
                    at_deps = []
                    for half in (0, 1):
                        sl = (sc % 2) * 2 + half
                        pieces = [(ctx[:, dv * 512 + sc * 128: dv * 512 + (sc + 1) * 128],
                                   wsb_o[:, dv * D + half * 512: dv * D + half * 512 + 512])
                                  for dv in range(KC)]
                        def ev(e, half=half, sl=sl, slot=slot, sc=sc):
                            return e.scalar_tensor_tensor(
                                out=x1b[slot][:, half * 512: half * 512 + 512],
                                in0=bigs(sl), scalar=recg[:, sc:sc + 1],
                                in1=x1b[slot][:, half * 512: half * 512 + 512],
                                op0=ALU.mult, op1=ALU.add)
                        dep, stp = mm_group(
                            bigs(sl), pieces,
                            waits=[(s, v) for (s, v) in ctx_deps] +
                                  [("sw", w_o), ("sx", xl)] +
                                  waitf(("recg2", sc)),
                            evac_fn=ev, evac_eng="vector", psum_key=("big", sl))
                        at_deps.append(dep)
                        wo_last = stp
                    setf(("recg", sc), *at_deps)
                    nd2 = ln_norm(x1b[slot][:], xn2[sc][:], sc % 4,
                                  [(s, v) for (s, v) in at_deps],
                                  dst_wait=waitf(("xn2", sc)), sum_eng="scalar")
                    pending.append(dict(seq=seq, g=g, sc=sc, ci=ci, slot=slot,
                                        nd2=nd2, row0=row0, w_base=w_base))
                    if sc <= 1:
                        ffn_head_dma(pending[-1])
                    if g == G - 1 and sc >= 2:
                        ffn_head(pending[sc - 2])
                setf(("ctx",), ("st", wo_last))
                addf(("xn1",), ("st", mix_last))
                if g == G - 1:
                    ffn_tail_base(pending[0])
                    ffn_hwt(pending[0])
                    ffn_tail_up(pending[0])
                    ffn_head_dma(pending[2])
                    ffn_head(pending[2])
                    ffn_tail_base(pending[1])
                    ffn_hwt(pending[1])
                    ffn_tail_up(pending[1])
                    ffn_head_dma(pending[3])
                    ffn_head(pending[3])
                    ffn_tail_base(pending[2])
                    ffn_hwt(pending[2])
                    ffn_tail_up(pending[2])
                    ffn_tail_base(pending[3])
                    ffn_hwt(pending[3])
                    ffn_tail_up(pending[3])

            nrow0 = (seq + 1) * S

            def bnd(c):
                if seq + 1 >= NSEQ:
                    return
                boundary_xl[c] = dma(
                    "sync", x1b[c][:],
                    x_d[nrow0 + c * 128: nrow0 + (c + 1) * 128, :],
                    waits=waitf(("x1", c)))
                nd = ln_norm(x1b[c][:], xn1[:, c * D:(c + 1) * D], c % 4,
                             [("sx", boundary_xl[c])],
                             dst_wait=waitf(("xn1",)))
                boundary_nds[c] = nd
                setf(("x1", c), nd)

            bnd(0)
            bnd(1)
            bnd(2)
            bnd(3)
            pending = []

        em.op("sync", lambda e: None, waits=[("so", em.now("so"))])

        block = st_.enter_context(nc.Block())

        def runner(qname):
            def run(eng):
                for ws, fn, inc, ib in em.q[qname]:
                    for s, v in ws:
                        eng.wait_ge(sems[s], v)
                    ins = fn(eng)
                    if inc is not None and ins is not None:
                        ins.then_inc(sems[inc], ib)
            return run

        block.tensor(runner("tensor"))
        block.vector(runner("vector"))
        block.scalar(runner("scalar"))
        block.sync(runner("sync"))
        block.gpsimd(runner("gpsimd"))

    return nc


def _host_layout(inputs, S, NSEQ):
    x = np.ascontiguousarray(np.asarray(inputs["L0_attn_x"], dtype=np.float32))
    B = x.shape[0]
    ncore = B // NSEQ
    wq = np.ascontiguousarray(np.asarray(inputs["L0_attn_Wq"], np.float32))
    wk = np.ascontiguousarray(np.asarray(inputs["L0_attn_Wk"], np.float32))
    wv = np.ascontiguousarray(np.asarray(inputs["L0_attn_Wv"], np.float32))
    wo = np.ascontiguousarray(np.asarray(inputs["L0_attn_Wo"], np.float32))
    wbase = np.ascontiguousarray(np.asarray(inputs["L0_ffn_base_W"], np.float32).T)
    shared_A = np.asarray(inputs["L0_ffn_shared_A"], np.float32)
    shared_B = np.asarray(inputs["L0_ffn_shared_B"], np.float32)
    routed_A = np.asarray(inputs["L0_ffn_routed_A"], np.float32)
    routed_B = np.asarray(inputs["L0_ffn_routed_B"], np.float32)
    router_W = np.asarray(inputs["L0_ffn_router_W"], np.float32)
    router_b = np.asarray(inputs["L0_ffn_router_b"], np.float32)
    base_b = np.asarray(inputs["L0_ffn_base_b"], np.float32)
    E = routed_A.shape[0]

    acat = np.zeros((D, 120), np.float32)
    acat[:, 0:16] = shared_A.T
    for e in range(E):
        acat[:, 16 + 16 * e: 32 + 16 * e] = routed_A[e].T
    acat[:, 112:118] = router_W.T
    acat = np.ascontiguousarray(acat)

    bcat = np.zeros((128, D), np.float32)
    bcat[0:16, :] = shared_B.T
    for e in range(E):
        bcat[16 + 16 * e: 32 + 16 * e, :] = routed_B[e].T
    bcat[112, :] = base_b
    bcat = np.ascontiguousarray(bcat)

    rb = np.zeros((128, 8), np.float32)
    rb[:, 0:6] = router_b[None, :]

    mask = np.zeros((128, 512), np.float32)
    ss, cc = np.meshgrid(np.arange(128), np.arange(512), indexing="ij")
    mask[cc > ss + 384] = MASKV
    mask = mask.astype(BF)

    ident = np.eye(128, dtype=BF)
    shared = {"wq": wq, "wk": wk, "wv": wv, "wo": wo, "wbase": wbase,
              "acat": acat, "bcat": bcat, "rb": rb, "mask": mask,
              "ident": ident}
    in_maps = []
    for c in range(ncore):
        m = dict(shared)
        m["x"] = np.ascontiguousarray(
            x[c * NSEQ:(c + 1) * NSEQ].reshape(NSEQ * S, D))
        in_maps.append(m)
    return in_maps, ncore


def kernel(**inputs):
    S = 2048
    NSEQ = 2
    in_maps, ncore = _host_layout(inputs, S, NSEQ)
    nc = build_program(S, NSEQ)
    res = run_bass_kernel_spmd(nc, in_maps, core_ids=list(range(ncore)))
    outs = [np.asarray(res.results[c]["out"]).reshape(NSEQ, S, D)
            for c in range(ncore)]
    return np.concatenate(outs, axis=0)


# version 37
# speedup vs baseline: 1.0399x; 1.0399x over previous
import math
import numpy as np
import ml_dtypes
import concourse.bass as bass
import concourse.mybir as mybir
from concourse.bass_utils import run_bass_kernel_spmd

F32 = mybir.dt.float32
BF16 = mybir.dt.bfloat16
AF = mybir.ActivationFunctionType
ALU = mybir.AluOpType
AX = mybir.AxisListType
BF = ml_dtypes.bfloat16

D = 1024
KC = 8
MASKV = -60.0

ENGINES = ("tensor", "vector", "scalar", "sync", "gpsimd")


class Em:
    def __init__(self):
        self.q = {e: [] for e in ENGINES}
        self.cnt = {}
        self.lastwait = {e: {} for e in ENGINES}

    def now(self, sem):
        return self.cnt.get(sem, 0)

    def op(self, eng, fn, waits=(), inc=None, inc_by=1):
        lw = self.lastwait[eng]
        ws = []
        for s, v in waits:
            if v is None or v <= 0:
                continue
            if v > lw.get(s, 0):
                lw[s] = v
                ws.append((s, v))
        if inc is not None:
            self.cnt[inc] = self.now(inc) + inc_by
        self.q[eng].append((tuple(ws), fn, inc, inc_by))
        return self.now(inc) if inc is not None else None


def ceil_div(a, b):
    return -(-a // b)


def build_program(S, NSEQ):
    G = S // 512
    NCH = S // 128
    NTC = S // 128
    nc = bass.Bass("TRN2", target_bir_lowering=False)

    x_d = nc.dram_tensor("x", [NSEQ * S, D], F32, kind="ExternalInput")
    wq_d = nc.dram_tensor("wq", [D, D], F32, kind="ExternalInput")
    wk_d = nc.dram_tensor("wk", [D, D], F32, kind="ExternalInput")
    wv_d = nc.dram_tensor("wv", [D, D], F32, kind="ExternalInput")
    wo_d = nc.dram_tensor("wo", [D, D], F32, kind="ExternalInput")
    wb_d = nc.dram_tensor("wbase", [D, D], F32, kind="ExternalInput")
    acat_d = nc.dram_tensor("acat", [D, 120], F32, kind="ExternalInput")
    bcat_d = nc.dram_tensor("bcat", [128, D], F32, kind="ExternalInput")
    rb_d = nc.dram_tensor("rb", [128, 8], F32, kind="ExternalInput")
    mask_d = nc.dram_tensor("mask", [128, 512], BF16, kind="ExternalInput")
    id_d = nc.dram_tensor("ident", [128, 128], BF16, kind="ExternalInput")
    out_d = nc.dram_tensor("out", [NSEQ * S, D], F32, kind="ExternalOutput")

    em = Em()
    from contextlib import ExitStack
    with ExitStack() as st_:
        def sb(name, shape, dtype):
            return st_.enter_context(nc.sbuf_tensor(name, shape, dtype))

        def psalloc(name, shape, dtype=F32):
            return st_.enter_context(nc.psum_tensor(name, shape, dtype))

        wsb_q = sb("wsb_q", [128, KC * D], BF16)
        wsb_kb = sb("wsb_kb", [128, KC * D], BF16)
        wsb_v = sb("wsb_v", [128, KC * D], BF16)
        wsb_o = sb("wsb_o", [128, KC * D], BF16)
        acat = sb("acat_sb", [128, KC * 120], BF16)
        bcat = sb("bcat_sb", [128, D], BF16)
        xn1 = sb("xn1_tm", [128, NCH * D], BF16)
        kt = sb("kt_fm", [128, KC * S], BF16)
        xnt = sb("xnt_g", [128, KC * 512], BF16)
        qmix = sb("qmix_g", [128, KC * 512], BF16)
        ctx = sb("ctx_g", [128, KC * 512], BF16)
        ptg = sb("pt_g", [128, NTC * 512], BF16)
        pbf = sb("p_bf", [128, S], BF16)
        trash = pbf
        x1b = [sb(f"x1_{i}", [128, D], F32) for i in range(4)]
        xn2 = [sb(f"xn2_{i}", [128, D], BF16) for i in range(4)]
        xn2t = [sb(f"xn2t_{i}", [128, KC * 128], BF16) for i in range(2)]
        hw = [sb(f"hw{i}", [128, 128], BF16) for i in range(2)]
        hwt = [sb(f"hwt{i}", [128, 128], BF16) for i in range(2)]
        mask = sb("mask_sb", [128, 512], BF16)
        ident = sb("ident_sb", [128, 128], BF16)
        rb = sb("rb_sb", [128, 8], F32)
        stats = [sb(f"stats{i}", [128, 8], F32) for i in range(4)]
        pmx = sb("pmx", [128, 4], F32)
        nmx = sb("nmx", [128, 4], F32)
        psum_s = sb("psum_s", [128, 4], F32)
        rsum = sb("rsum", [128, 1], F32)
        recg = sb("recg", [128, 4], F32)
        epst = sb("epst", [128, 1], F32)
        eps6 = sb("eps6", [128, 1], F32)
        lg = [sb(f"lg{i}", [128, 8], F32) for i in range(2)]
        ge6 = [sb(f"ge6_{i}", [128, 8], F32) for i in range(2)]
        se6 = [sb(f"se6_{i}", [128, 1], F32) for i in range(2)]
        top8 = [sb(f"top8_{i}", [128, 8], F32) for i in range(2)]
        den = [sb(f"den{i}", [128, 1], F32) for i in range(2)]
        w6 = [sb(f"w6_{i}", [128, 8], F32) for i in range(2)]
        g_nm = sb("g_nm", [128, 1], F32)
        h_sb = [sb(f"h_sb{i}", [128, 118], F32) for i in range(2)]

        big = psalloc("ps_big", [128, 2048])
        aux = psalloc("ps_aux", [128, 1024])
        psh = psalloc("ps_h", [128, 128])
        tpa = psalloc("ps_tpa", [128, 128], BF16)

        try:
            rem = nc.sbuf_bytes_remaining
            print(f"[kernel build] sbuf bytes remaining/partition: {rem}")
        except Exception:
            pass

        sems = {}
        for s in ("sx", "sw", "so", "sd", "sd2", "st", "sv", "ss", "sg"):
            sems[s] = st_.enter_context(nc.semaphore(s))

        free = {}

        def waitf(key):
            return list(free.get(key, ()))

        def setf(key, *deps):
            free[key] = [d for d in deps if d is not None]

        def addf(key, *deps):
            free.setdefault(key, [])
            free[key].extend(d for d in deps if d is not None)

        def dma(eng, dst, src, waits=(), sem="sx"):
            return em.op(eng, lambda e, d=dst, s=src: e.dma_start(out=d, in_=s),
                         waits=waits, inc=sem, inc_by=16)

        def dmat(dst, src, waits=(), eng="sync"):
            sem = "sd" if eng == "sync" else "sd2"
            return em.op(eng,
                         lambda e, d=dst, s=src: e.dma_start_transpose(d, s),
                         waits=waits, inc=sem, inc_by=16)

        def vop(fn, waits=(), inc="sv"):
            return em.op("vector", fn, waits=waits, inc=inc)

        def sop(fn, waits=(), inc="ss"):
            return em.op("scalar", fn, waits=waits, inc=inc)

        def gop(fn, waits=(), inc="sg"):
            return em.op("gpsimd", fn, waits=waits, inc=inc)

        def top(fn, waits=(), inc="st"):
            return em.op("tensor", fn, waits=waits, inc=inc)

        c_mask = dma("sync", mask[:], mask_d[:])
        c_id = dma("sync", ident[:], id_d[:])
        c_rb = dma("sync", rb[:], rb_d[:])

        def wload(dst, src_d, waits=()):
            return dma("gpsimd", dst.rearrange("p (k n) -> p k n", k=KC),
                       src_d.rearrange("(k p) n -> p k n", p=128), waits=waits,
                       sem="sw")

        def wk_load(waits=()):
            dma("gpsimd", qmix.rearrange("p (k n) -> p k n", k=4),
                wk_d[0:512, :].rearrange("(k p) n -> p k n", p=128),
                waits=list(waits), sem="sw")
            return dma("gpsimd", ctx.rearrange("p (k n) -> p k n", k=4),
                       wk_d[512:1024, :].rearrange("(k p) n -> p k n", p=128),
                       waits=list(waits), sem="sw")

        def wk_sl(k, m):
            buf = qmix if k < 4 else ctx
            return buf[:, (k % 4) * D + m * 128: (k % 4) * D + (m + 1) * 128]

        w_kb = wk_load()
        w_base = wload(wsb_kb, wb_d)
        w_q = wload(wsb_q, wq_d)
        w_v = wload(wsb_v, wv_d)
        w_o = wload(wsb_o, wo_d)
        w_ac = wload(acat, acat_d)
        w_bc = dma("gpsimd", bcat[:], bcat_d[:], sem="sw")

        vop(lambda e: e.memset(epst[:], 1e-5))
        vop(lambda e: e.memset(eps6[:], 1e-6))
        for i in range(2):
            vop(lambda e, i=i: e.memset(ge6[i][:, 6:8], -1e30))
        for i in range(2):
            vop(lambda e, i=i: e.memset(hw[i][:, 112:113], 1.0))
            vop(lambda e, i=i: e.memset(hw[i][:, 113:128], 0.0))

        def bigs(i):
            return big[:, i * 512:(i + 1) * 512]

        def ln_norm(src, dst, par, xwait, dst_wait=(), sum_eng="vector"):
            stt = stats[par]
            sq_done = sop(lambda e, s=src: e.activation(
                out=trash[:, 0:D], in_=s, func=AF.Square, accum_out=stt[:, 1:2]),
                waits=list(xwait) + waitf(("stats", par)))
            if sum_eng == "scalar":
                sum_done = sop(lambda e, s=src: e.activation(
                    out=trash[:, 0:D], in_=s, func=AF.Copy,
                    accum_out=stt[:, 0:1]))
                sdep = ("ss", sum_done)
            else:
                sum_done = vop(lambda e, s=src: e.reduce_sum(
                    stt[:, 0:1], s, axis=AX.X), waits=list(xwait))
                sdep = ("sv", sum_done)
            m_done = vop(lambda e: e.tensor_scalar(
                out=stt[:, 2:3], in0=stt[:, 0:1], scalar1=1.0 / D, scalar2=None,
                op0=ALU.mult), waits=[sdep])
            nv_done = vop(lambda e: e.scalar_tensor_tensor(
                out=stt[:, 4:5], in0=stt[:, 2:3], scalar=stt[:, 0:1],
                in1=stt[:, 1:2], op0=ALU.mult, op1=ALU.subtract),
                waits=[("sv", m_done), ("ss", sq_done)])
            ln_done = sop(lambda e: e.activation(
                out=stt[:, 5:6], in_=stt[:, 4:5], func=AF.Ln, bias=epst[:],
                scale=-1.0 / D), waits=[("sv", nv_done)])
            rstd_done = sop(lambda e: e.activation(
                out=stt[:, 6:7], in_=stt[:, 5:6], func=AF.Exp, scale=-0.5))
            norm_done = vop(lambda e, s=src, d=dst: e.tensor_scalar(
                out=d, in0=s, scalar1=stt[:, 2:3], scalar2=stt[:, 6:7],
                op0=ALU.subtract, op1=ALU.mult),
                waits=[("ss", rstd_done)] + list(dst_wait))
            setf(("stats", par), ("sv", norm_done))
            return ("sv", norm_done)

        def hw_transpose(slot, src_ready):
            t_done = top(lambda e, s=slot: e.transpose(
                tpa[:], hw[s][:], ident[:]),
                waits=list(src_ready) + waitf(("tpa",)) + [("sx", c_id)])
            ev = vop(lambda e, s=slot: e.tensor_copy(hwt[s][:], tpa[:]),
                     waits=[("st", t_done)])
            setf(("tpa",), ("sv", ev))
            return ("sv", ev)

        def mm_group(out_ap, pieces, waits, evac_fn, evac_eng="scalar",
                     psum_key=None):
            n = len(pieces)
            stp = None
            for i, (l, r) in enumerate(pieces):
                w = list(waits) + (waitf(psum_key) if i == 0 else [])
                stp = top(lambda e, l=l, r=r, i=i, o=out_ap: e.matmul(
                    o, l, r, start=(i == 0), stop=(i == n - 1)),
                    waits=w if i == 0 else [], inc="st" if i == n - 1 else None)
            if evac_eng == "scalar":
                ev = sop(lambda e: evac_fn(e), waits=[("st", stp)])
                dep = ("ss", ev)
            else:
                ev = vop(lambda e: evac_fn(e), waits=[("st", stp)])
                dep = ("sv", ev)
            if psum_key is not None:
                setf(psum_key, dep)
            return dep, stp

        nstore = [0]
        base_last = [0]

        def ffn_head_dma(fc):
            sc = fc["sc"]
            ts = sc % 2
            tdp = dmat(
                xn2t[ts].rearrange("p (k s) -> p k s", s=128),
                xn2[sc].rearrange("p (k d) -> p k d", d=128),
                waits=[fc["nd2"]] + waitf(("xn2t", ts)))
            free.pop(("xn2t", ts), None)
            setf(("xn2", sc), ("sd", tdp))
            fc["t2"] = [("sd", tdp)]
            fc["ts"] = ts

        def ffn_head(fc):
            sc, slot = fc["sc"], fc["slot"]
            ts = fc["ts"]
            t2 = fc["t2"]
            h_stp = None
            for k in range(KC):
                w = (t2 + [("sw", w_ac)] + waitf(("psh",))) if k == 0 else []
                h_stp = top(lambda e, k=k, ts=ts: e.matmul(
                    psh[:, 0:120], xn2t[ts][:, k * 128:(k + 1) * 128],
                    acat[:, k * 120:(k + 1) * 120],
                    start=(k == 0), stop=(k == KC - 1)),
                    waits=w, inc="st" if k == KC - 1 else None)
            fc["h_stp"] = h_stp
            hp = sc % 2
            hev = sop(lambda e, hp=hp: e.copy(h_sb[hp][:], psh[:, 0:118]),
                      waits=[("st", h_stp)] + waitf(("h_sb", hp)))
            setf(("psh",), ("ss", hev))
            p = sc % 2
            d1 = vop(lambda e, p=p: e.tensor_tensor(
                lg[p][:, 0:6], h_sb[p][:, 112:118], rb[:, 0:6], ALU.add),
                waits=[("ss", hev), ("sx", c_rb)] + waitf(("lg", p)))
            d3 = sop(lambda e, p=p: e.activation(
                out=ge6[p][:, 0:6], in_=lg[p][:, 0:6], func=AF.Exp,
                accum_out=se6[p][:]), waits=[("sv", d1)])
            setf(("lg", p), ("ss", d3))
            d6 = vop(lambda e, p=p: e.max(out=top8[p][:], in_=ge6[p][:]),
                     waits=[("ss", d3)])
            d7 = vop(lambda e, p=p: e.scalar_tensor_tensor(
                out=ge6[p][:, 0:6], in0=ge6[p][:, 0:6], scalar=top8[p][:, 1:2],
                in1=ge6[p][:, 0:6], op0=ALU.is_ge, op1=ALU.mult),
                waits=[("sv", d6)])
            d8 = vop(lambda e, p=p: e.scalar_tensor_tensor(
                out=den[p][:], in0=se6[p][:], scalar=1e-6, in1=top8[p][:, 0:1],
                op0=ALU.mult, op1=ALU.add))
            d8b = vop(lambda e, p=p: e.tensor_scalar(
                out=den[p][:], in0=den[p][:], scalar1=top8[p][:, 1:2],
                scalar2=None, op0=ALU.add))
            d9 = vop(lambda e, p=p: e.reciprocal(den[p][:], den[p][:]),
                     waits=[("sv", d8b)])
            d10 = vop(lambda e, p=p: e.tensor_scalar(
                out=w6[p][:, 0:6], in0=ge6[p][:, 0:6], scalar1=den[p][:],
                scalar2=None, op0=ALU.mult), waits=[("sv", d9), ("sv", d7)])
            hb_last = vop(lambda e, slot=slot, p=p: e.tensor_copy(
                hw[slot % 2][:, 0:16], h_sb[p][:, 0:16]),
                waits=waitf(("hw", slot % 2)))
            for ex in range(6):
                hb_last = vop(lambda e, ex=ex, slot=slot, p=p: e.tensor_scalar(
                    out=hw[slot % 2][:, 16 + 16 * ex: 32 + 16 * ex],
                    in0=h_sb[p][:, 16 + 16 * ex: 32 + 16 * ex],
                    scalar1=w6[p][:, ex:ex + 1], scalar2=None, op0=ALU.mult),
                    waits=[("sv", d10)] if ex == 0 else [])
            free.pop(("hw", slot % 2), None)
            setf(("h_sb", p), ("sv", hb_last))
            fc["hb_deps"] = [("sv", hb_last)]

        def ffn_tail_base(fc):
            ts = fc["ts"]
            stp = None
            for half in (0, 1):
                for k in range(KC):
                    w = []
                    if k == 0:
                        w = (fc["t2"] + [("sw", fc["w_base"])] +
                             waitf(("aux", half)))
                    last = (half == 1 and k == KC - 1)
                    stp = top(lambda e, k=k, half=half, ts=ts: e.matmul(
                        aux[:, half * 512: half * 512 + 512],
                        xn2t[ts][:, k * 128:(k + 1) * 128],
                        wsb_kb[:, k * D + half * 512: k * D + half * 512 + 512],
                        start=(k == 0), stop=False), waits=w,
                        inc="st" if last else None)
            setf(("xn2t", ts), ("st", stp))

        def ffn_hwt(fc):
            slot = fc["slot"]
            fc["hwt_dep"] = hw_transpose(slot % 2, list(fc["hb_deps"]) +
                                         waitf(("hwt", slot % 2)))
            free.pop(("hwt", slot % 2), None)
            setf(("hw", slot % 2), fc["hwt_dep"])

        def ffn_tail_up(fc):
            seq, g, sc, slot, ts = fc["seq"], fc["g"], fc["sc"], fc["slot"], fc["ts"]
            ci = fc["ci"]
            row0 = fc["row0"]
            ffn_deps = []
            for half in (0, 1):
                stp = top(lambda e, half=half, ts=ts: e.matmul(
                    aux[:, half * 512: half * 512 + 512], hwt[slot % 2][:],
                    bcat[:, half * 512: half * 512 + 512],
                    start=False, stop=True),
                    waits=[fc["hwt_dep"], ("sw", w_bc)], inc="st")
                base_last[0] = stp
                evd = vop(lambda e, half=half, slot=slot: e.tensor_add(
                    x1b[slot][:, half * 512: half * 512 + 512],
                    x1b[slot][:, half * 512: half * 512 + 512],
                    aux[:, half * 512: half * 512 + 512]),
                    waits=[("st", stp), fc["nd2"]])
                ffn_deps.append(("sv", evd))
                setf(("aux", half), ("sv", evd))
            addf(("hwt", slot % 2), *ffn_deps)
            so_c = dma("sync", out_d[row0 + ci * 128: row0 + (ci + 1) * 128, :],
                       x1b[slot][:], waits=ffn_deps, sem="so")
            nstore[0] += 1
            setf(("x1", slot), ("so", so_c))

        boundary_nds = {}
        boundary_xl = {}
        for seq in range(NSEQ):
            row0 = seq * S
            ln1_done = [None] * NCH
            if seq > 0:
                w_kb = wk_load(waits=waitf(("qmix",)) + waitf(("ctx",)))

            kproj_last = 0
            xlv = {}

            def emit_loads(g):
                for c in range(4):
                    ci = g * 4 + c
                    slot = ci % 4
                    xlv[ci] = dma(
                        "sync", x1b[slot][:],
                        x_d[row0 + ci * 128: row0 + (ci + 1) * 128, :],
                        waits=waitf(("x1", slot)))

            def emit_lns(g):
                out = []
                for c in range(4):
                    ci = g * 4 + c
                    slot = ci % 4
                    nd = ln_norm(x1b[slot][:], xn1[:, ci * D:(ci + 1) * D],
                                 ci % 4, [("sx", xlv[ci])],
                                 dst_wait=waitf(("xn1",)))
                    ln1_done[ci] = nd
                    setf(("x1", slot), nd)
                    out.append(nd)
                return out

            if boundary_nds:
                nds = [boundary_nds[c] for c in range(4)]
                for c in range(4):
                    ln1_done[c] = nds[c]
                boundary_nds = {}
                boundary_xl = {}
            else:
                emit_loads(0)
                nds = emit_lns(0)
            for g in range(G):
                if g + 1 < G:
                    emit_loads(g + 1)
                td = []
                for c in range(4):
                    ci = g * 4 + c
                    tdp = dmat(
                        xnt.rearrange("p (k s) -> p k s", s=512)[:, :, c * 128:(c + 1) * 128],
                        xn1.rearrange("p (c k d) -> p c k d", c=NCH, k=KC)[:, ci],
                        waits=[nds[c]] + waitf(("xnt",)))
                    td.append(("sd", tdp))
                if g + 1 < G:
                    nds = emit_lns(g + 1)
                for m in range(KC):
                    sl = m % 4
                    pieces = [(wk_sl(k, m),
                               xnt[:, k * 512:(k + 1) * 512]) for k in range(KC)]
                    dst = kt[:, m * S + g * 512: m * S + (g + 1) * 512]
                    dep, stp = mm_group(
                        bigs(sl), pieces, waits=td + [("sw", w_kb)],
                        evac_fn=(lambda e, d=dst, sl=sl: e.copy(d, bigs(sl))),
                        psum_key=("big", sl))
                    kproj_last = stp
                setf(("xnt",), ("st", kproj_last))
            k_evac_ss = em.now("ss")
            setf(("qmix",), ("st", kproj_last))
            setf(("ctx",), ("st", kproj_last))

            def emit_qtp(g):
                td = []
                for c in range(4):
                    ci = g * 4 + c
                    tdp = dmat(
                        xnt.rearrange("p (k s) -> p k s", s=512)[:, :, c * 128:(c + 1) * 128],
                        xn1.rearrange("p (c k d) -> p c k d", c=NCH, k=KC)[:, ci],
                        waits=[ln1_done[ci]] + waitf(("xnt",)))
                    td.append(("sd", tdp))
                return td

            pending = []
            td_next = emit_qtp(0)
            for g in range(G):
                td = td_next
                q_deps = []
                q_last = 0
                for m in range(KC):
                    sl = m % 4
                    pieces = [(wsb_q[:, k * D + m * 128: k * D + (m + 1) * 128],
                               xnt[:, k * 512:(k + 1) * 512]) for k in range(KC)]
                    dst = qmix[:, m * 512:(m + 1) * 512]
                    dep, stp = mm_group(
                        bigs(sl), pieces,
                        waits=td + [("sw", w_q)] + waitf(("qmix",)),
                        evac_fn=(lambda e, d=dst, sl=sl: e.activation(
                            out=d, in_=bigs(sl), func=AF.Copy, scale=1.0 / 32.0)),
                        psum_key=("big", sl))
                    q_deps.append(dep)
                    q_last = stp
                free.pop(("qmix",), None)
                setf(("xnt",), ("st", q_last))
                if g + 1 < G:
                    td_next = emit_qtp(g + 1)
                pz = vop(lambda e, g=g: e.memset(
                    ptg[:, g * 4 * 512:(g * 4 + 4) * 512], 0.0),
                    waits=waitf(("ptg",)))
                free.pop(("ptg",), None)

                pt_sd = []
                exp_all = 0
                for qt in range(4):
                    qg = g * 4 + qt
                    tmax = (qg + 1) * 128
                    nt = ceil_div(tmax, 512)
                    boff = 2 * (qt % 2) if nt <= 2 else 0
                    stps = []
                    pm_deps = []
                    if nt >= 3:
                        tc_order = [0, nt - 1] + list(range(1, nt - 1))
                    else:
                        tc_order = list(range(nt))
                    for tc in tc_order:
                        w = min(512, tmax - tc * 512)
                        stp = None
                        for k in range(KC):
                            ww = []
                            if k == 0:
                                ww = waitf(("big", boff + tc))
                                if tc == 0:
                                    ww = ww + [("ss", k_evac_ss)] + \
                                        [(s, v) for (s, v) in q_deps]
                            stp = top(lambda e, k=k, tc=tc, w=w, qt=qt, boff=boff: e.matmul(
                                big[:, (boff + tc) * 512: (boff + tc) * 512 + w],
                                qmix[:, k * 512 + qt * 128: k * 512 + (qt + 1) * 128],
                                kt[:, k * S + tc * 512: k * S + tc * 512 + w],
                                start=(k == 0), stop=(k == KC - 1)),
                                waits=ww, inc="st" if k == KC - 1 else None)
                        stps.append(stp)
                        last_emitted = (tc == tc_order[-1])
                        if last_emitted and pending:
                            if qt >= 1:
                                ffn_tail_base(pending[qt - 1])
                                if qt + 1 < 4:
                                    ffn_head_dma(pending[qt + 1])
                        if tc == nt - 1:
                            off = qt * 128
                            mk = vop(lambda e, tc=tc, w=w, off=off, boff=boff: e.tensor_tensor(
                                big[:, (boff + tc) * 512: (boff + tc) * 512 + w],
                                big[:, (boff + tc) * 512: (boff + tc) * 512 + w],
                                mask[:, 384 - off: 384 - off + w], ALU.add),
                                waits=[("st", stp), ("sx", c_mask)])
                            pm = vop(lambda e, tc=tc, w=w, boff=boff: e.reduce_max(
                                pmx[:, tc:tc + 1],
                                big[:, (boff + tc) * 512: (boff + tc) * 512 + w],
                                axis=AX.X), waits=[("sv", mk)])
                        else:
                            pm = vop(lambda e, tc=tc, w=w, boff=boff: e.reduce_max(
                                pmx[:, tc:tc + 1],
                                big[:, (boff + tc) * 512: (boff + tc) * 512 + w],
                                axis=AX.X), waits=[("st", stp)] + waitf(("pmx",)))
                        pm_deps.append(("sv", pm))
                    nm_done = vop(lambda e, nt=nt, qt=qt: e.reduce_max(
                        nmx[:, qt:qt + 1], pmx[:, 0:nt], axis=AX.X, negate=True),
                        waits=pm_deps + waitf(("nmx", qt)))
                    free.pop(("nmx", qt), None)
                    setf(("pmx",), ("sv", nm_done))
                    e_done = 0
                    for tc in range(nt):
                        w = min(512, tmax - tc * 512)
                        e_done = sop(lambda e, tc=tc, w=w, qt=qt, boff=boff: e.activation(
                            out=pbf[:, tc * 512: tc * 512 + w],
                            in_=big[:, (boff + tc) * 512: (boff + tc) * 512 + w],
                            func=AF.Exp, bias=nmx[:, qt:qt + 1],
                            accum_out=psum_s[:, tc:tc + 1]),
                            waits=[("sv", nm_done)] + waitf(("pbf",)) +
                                  waitf(("psum_s",)))
                        setf(("big", boff + tc), ("ss", e_done))
                    free.pop(("pbf",), None)
                    free.pop(("psum_s",), None)
                    exp_all = e_done
                    setf(("nmx", qt), ("ss", e_done))
                    if pending and qt >= 1:
                        ffn_hwt(pending[qt - 1])
                    rs = vop(lambda e, nt=nt: e.reduce_sum(
                        rsum[:], psum_s[:, 0:nt], axis=AX.X),
                        waits=[("ss", e_done)])
                    setf(("psum_s",), ("sv", rs))
                    rc = vop(lambda e, qt=qt: e.reciprocal(
                        recg[:, qt:qt + 1], rsum[:]),
                        waits=[("sv", rs)] + waitf(("recg", qt)))
                    free.pop(("recg", qt), None)
                    setf(("recg2", qt), ("sv", rc))
                    ntc2 = qg + 1
                    pt = dmat(
                        ptg.rearrange("p (c q) -> p c q", q=512)[:, 0:ntc2, qt * 128:(qt + 1) * 128],
                        pbf.rearrange("p (c t) -> p c t", t=128)[:, 0:ntc2, :],
                        waits=[("ss", e_done), ("sv", pz)] + waitf(("ptg_w",)))
                    pt_sd.append(pt)
                    setf(("pbf",), ("sd", pt))
                    if pending:
                        if qt >= 1:
                            ffn_tail_up(pending[qt - 1])
                        ffn_head(pending[qt])
                if pending:
                    ffn_tail_base(pending[3])
                    ffn_hwt(pending[3])
                    ffn_tail_up(pending[3])
                    pending = []

                xlb = {}
                for sc in range(4):
                    ci = g * 4 + sc
                    xlb[sc] = dma("sync", x1b[sc][:],
                                  x_d[row0 + ci * 128: row0 + (ci + 1) * 128, :],
                                  waits=waitf(("x1", sc)))
                ntc2 = (g + 1) * 4
                mix_deps = []
                mix_last = 0
                for dch in range(KC):
                    sl = dch % 4
                    pieces = [(xn1[:, tc2 * D + dch * 128: tc2 * D + (dch + 1) * 128],
                               ptg[:, tc2 * 512:(tc2 + 1) * 512])
                              for tc2 in range(ntc2)]
                    dst = qmix[:, dch * 512:(dch + 1) * 512]
                    dep, stp = mm_group(
                        bigs(sl), pieces,
                        waits=[("sd", pt_sd[-1]), ("ss", exp_all)],
                        evac_fn=(lambda e, d=dst, sl=sl: e.copy(d, bigs(sl))),
                        psum_key=("big", sl))
                    mix_deps.append(dep)
                    mix_last = stp
                setf(("ptg_w",), ("st", mix_last))
                setf(("ptg",), ("st", mix_last))
                ctx_deps = []
                ctx_last = 0
                for dv in range(KC):
                    sl = dv % 4
                    pieces = [(wsb_v[:, k * D + dv * 128: k * D + (dv + 1) * 128],
                               qmix[:, k * 512:(k + 1) * 512]) for k in range(KC)]
                    dst = ctx[:, dv * 512:(dv + 1) * 512]
                    dep, stp = mm_group(
                        bigs(sl), pieces,
                        waits=[(s, v) for (s, v) in mix_deps] + [("sw", w_v)] +
                              waitf(("ctx",)),
                        evac_fn=(lambda e, d=dst, sl=sl: e.copy(d, bigs(sl))),
                        psum_key=("big", sl))
                    ctx_deps.append(dep)
                    ctx_last = stp
                free.pop(("ctx",), None)
                setf(("qmix",), ("st", ctx_last))

                wo_last = 0
                for sc in range(4):
                    ci = g * 4 + sc
                    slot = sc
                    xl = xlb[sc]
                    at_deps = []
                    for half in (0, 1):
                        sl = (sc % 2) * 2 + half
                        pieces = [(ctx[:, dv * 512 + sc * 128: dv * 512 + (sc + 1) * 128],
                                   wsb_o[:, dv * D + half * 512: dv * D + half * 512 + 512])
                                  for dv in range(KC)]
                        def ev(e, half=half, sl=sl, slot=slot, sc=sc):
                            return e.scalar_tensor_tensor(
                                out=x1b[slot][:, half * 512: half * 512 + 512],
                                in0=bigs(sl), scalar=recg[:, sc:sc + 1],
                                in1=x1b[slot][:, half * 512: half * 512 + 512],
                                op0=ALU.mult, op1=ALU.add)
                        dep, stp = mm_group(
                            bigs(sl), pieces,
                            waits=[(s, v) for (s, v) in ctx_deps] +
                                  [("sw", w_o), ("sx", xl)] +
                                  waitf(("recg2", sc)),
                            evac_fn=ev, evac_eng="vector", psum_key=("big", sl))
                        at_deps.append(dep)
                        wo_last = stp
                    setf(("recg", sc), *at_deps)
                    nd2 = ln_norm(x1b[slot][:], xn2[sc][:], sc % 4,
                                  [(s, v) for (s, v) in at_deps],
                                  dst_wait=waitf(("xn2", sc)))
                    pending.append(dict(seq=seq, g=g, sc=sc, ci=ci, slot=slot,
                                        nd2=nd2, row0=row0, w_base=w_base))
                    if sc <= 1:
                        ffn_head_dma(pending[-1])
                    if g == G - 1 and sc >= 2:
                        ffn_head(pending[sc - 2])
                setf(("ctx",), ("st", wo_last))
                addf(("xn1",), ("st", mix_last))
                if g == G - 1:
                    ffn_tail_base(pending[0])
                    ffn_hwt(pending[0])
                    ffn_tail_up(pending[0])
                    ffn_head_dma(pending[2])
                    ffn_head(pending[2])
                    ffn_tail_base(pending[1])
                    ffn_hwt(pending[1])
                    ffn_tail_up(pending[1])
                    ffn_head_dma(pending[3])
                    ffn_head(pending[3])
                    ffn_tail_base(pending[2])
                    ffn_hwt(pending[2])
                    ffn_tail_up(pending[2])
                    ffn_tail_base(pending[3])
                    ffn_hwt(pending[3])
                    ffn_tail_up(pending[3])

            nrow0 = (seq + 1) * S

            def bnd(c):
                if seq + 1 >= NSEQ:
                    return
                boundary_xl[c] = dma(
                    "sync", x1b[c][:],
                    x_d[nrow0 + c * 128: nrow0 + (c + 1) * 128, :],
                    waits=waitf(("x1", c)))
                nd = ln_norm(x1b[c][:], xn1[:, c * D:(c + 1) * D], c % 4,
                             [("sx", boundary_xl[c])],
                             dst_wait=waitf(("xn1",)))
                boundary_nds[c] = nd
                setf(("x1", c), nd)

            bnd(0)
            bnd(1)
            bnd(2)
            bnd(3)
            pending = []

        em.op("sync", lambda e: None, waits=[("so", em.now("so"))])

        block = st_.enter_context(nc.Block())

        def runner(qname):
            def run(eng):
                for ws, fn, inc, ib in em.q[qname]:
                    for s, v in ws:
                        eng.wait_ge(sems[s], v)
                    ins = fn(eng)
                    if inc is not None and ins is not None:
                        ins.then_inc(sems[inc], ib)
            return run

        block.tensor(runner("tensor"))
        block.vector(runner("vector"))
        block.scalar(runner("scalar"))
        block.sync(runner("sync"))
        block.gpsimd(runner("gpsimd"))

    return nc


def _host_layout(inputs, S, NSEQ):
    x = np.ascontiguousarray(np.asarray(inputs["L0_attn_x"], dtype=np.float32))
    B = x.shape[0]
    ncore = B // NSEQ
    wq = np.ascontiguousarray(np.asarray(inputs["L0_attn_Wq"], np.float32))
    wk = np.ascontiguousarray(np.asarray(inputs["L0_attn_Wk"], np.float32))
    wv = np.ascontiguousarray(np.asarray(inputs["L0_attn_Wv"], np.float32))
    wo = np.ascontiguousarray(np.asarray(inputs["L0_attn_Wo"], np.float32))
    wbase = np.ascontiguousarray(np.asarray(inputs["L0_ffn_base_W"], np.float32).T)
    shared_A = np.asarray(inputs["L0_ffn_shared_A"], np.float32)
    shared_B = np.asarray(inputs["L0_ffn_shared_B"], np.float32)
    routed_A = np.asarray(inputs["L0_ffn_routed_A"], np.float32)
    routed_B = np.asarray(inputs["L0_ffn_routed_B"], np.float32)
    router_W = np.asarray(inputs["L0_ffn_router_W"], np.float32)
    router_b = np.asarray(inputs["L0_ffn_router_b"], np.float32)
    base_b = np.asarray(inputs["L0_ffn_base_b"], np.float32)
    E = routed_A.shape[0]

    acat = np.zeros((D, 120), np.float32)
    acat[:, 0:16] = shared_A.T
    for e in range(E):
        acat[:, 16 + 16 * e: 32 + 16 * e] = routed_A[e].T
    acat[:, 112:118] = router_W.T
    acat = np.ascontiguousarray(acat)

    bcat = np.zeros((128, D), np.float32)
    bcat[0:16, :] = shared_B.T
    for e in range(E):
        bcat[16 + 16 * e: 32 + 16 * e, :] = routed_B[e].T
    bcat[112, :] = base_b
    bcat = np.ascontiguousarray(bcat)

    rb = np.zeros((128, 8), np.float32)
    rb[:, 0:6] = router_b[None, :]

    mask = np.zeros((128, 512), np.float32)
    ss, cc = np.meshgrid(np.arange(128), np.arange(512), indexing="ij")
    mask[cc > ss + 384] = MASKV
    mask = mask.astype(BF)

    ident = np.eye(128, dtype=BF)
    shared = {"wq": wq, "wk": wk, "wv": wv, "wo": wo, "wbase": wbase,
              "acat": acat, "bcat": bcat, "rb": rb, "mask": mask,
              "ident": ident}
    in_maps = []
    for c in range(ncore):
        m = dict(shared)
        m["x"] = np.ascontiguousarray(
            x[c * NSEQ:(c + 1) * NSEQ].reshape(NSEQ * S, D))
        in_maps.append(m)
    return in_maps, ncore


def kernel(**inputs):
    S = 2048
    NSEQ = 2
    in_maps, ncore = _host_layout(inputs, S, NSEQ)
    nc = build_program(S, NSEQ)
    res = run_bass_kernel_spmd(nc, in_maps, core_ids=list(range(ncore)))
    outs = [np.asarray(res.results[c]["out"]).reshape(NSEQ, S, D)
            for c in range(ncore)]
    return np.concatenate(outs, axis=0)


# version 38
# speedup vs baseline: 1.0409x; 1.0009x over previous
import math
import numpy as np
import ml_dtypes
import concourse.bass as bass
import concourse.mybir as mybir
from concourse.bass_utils import run_bass_kernel_spmd

F32 = mybir.dt.float32
BF16 = mybir.dt.bfloat16
AF = mybir.ActivationFunctionType
ALU = mybir.AluOpType
AX = mybir.AxisListType
BF = ml_dtypes.bfloat16

D = 1024
KC = 8
MASKV = -60.0

ENGINES = ("tensor", "vector", "scalar", "sync", "gpsimd")


class Em:
    def __init__(self):
        self.q = {e: [] for e in ENGINES}
        self.cnt = {}
        self.lastwait = {e: {} for e in ENGINES}

    def now(self, sem):
        return self.cnt.get(sem, 0)

    def op(self, eng, fn, waits=(), inc=None, inc_by=1):
        lw = self.lastwait[eng]
        ws = []
        for s, v in waits:
            if v is None or v <= 0:
                continue
            if v > lw.get(s, 0):
                lw[s] = v
                ws.append((s, v))
        if inc is not None:
            self.cnt[inc] = self.now(inc) + inc_by
        self.q[eng].append((tuple(ws), fn, inc, inc_by))
        return self.now(inc) if inc is not None else None


def ceil_div(a, b):
    return -(-a // b)


def build_program(S, NSEQ):
    G = S // 512
    NCH = S // 128
    NTC = S // 128
    nc = bass.Bass("TRN2", target_bir_lowering=False)

    x_d = nc.dram_tensor("x", [NSEQ * S, D], F32, kind="ExternalInput")
    wq_d = nc.dram_tensor("wq", [D, D], F32, kind="ExternalInput")
    wk_d = nc.dram_tensor("wk", [D, D], F32, kind="ExternalInput")
    wv_d = nc.dram_tensor("wv", [D, D], F32, kind="ExternalInput")
    wo_d = nc.dram_tensor("wo", [D, D], F32, kind="ExternalInput")
    wb_d = nc.dram_tensor("wbase", [D, D], F32, kind="ExternalInput")
    acat_d = nc.dram_tensor("acat", [D, 120], F32, kind="ExternalInput")
    bcat_d = nc.dram_tensor("bcat", [128, D], F32, kind="ExternalInput")
    rb_d = nc.dram_tensor("rb", [128, 8], F32, kind="ExternalInput")
    mask_d = nc.dram_tensor("mask", [128, 512], BF16, kind="ExternalInput")
    id_d = nc.dram_tensor("ident", [128, 128], BF16, kind="ExternalInput")
    out_d = nc.dram_tensor("out", [NSEQ * S, D], F32, kind="ExternalOutput")

    em = Em()
    from contextlib import ExitStack
    with ExitStack() as st_:
        def sb(name, shape, dtype):
            return st_.enter_context(nc.sbuf_tensor(name, shape, dtype))

        def psalloc(name, shape, dtype=F32):
            return st_.enter_context(nc.psum_tensor(name, shape, dtype))

        wsb_q = sb("wsb_q", [128, KC * D], BF16)
        wsb_kb = sb("wsb_kb", [128, KC * D], BF16)
        wsb_v = sb("wsb_v", [128, KC * D], BF16)
        wsb_o = sb("wsb_o", [128, KC * D], BF16)
        acat = sb("acat_sb", [128, KC * 120], BF16)
        bcat = sb("bcat_sb", [128, D], BF16)
        xn1 = sb("xn1_tm", [128, NCH * D], BF16)
        kt = sb("kt_fm", [128, KC * S], BF16)
        xnt = sb("xnt_g", [128, KC * 512], BF16)
        qmix = sb("qmix_g", [128, KC * 512], BF16)
        ctx = sb("ctx_g", [128, KC * 512], BF16)
        ptg = sb("pt_g", [128, NTC * 512], BF16)
        pbf = sb("p_bf", [128, S], BF16)
        trash = pbf
        x1b = [sb(f"x1_{i}", [128, D], F32) for i in range(4)]
        xn2 = [sb(f"xn2_{i}", [128, D], BF16) for i in range(4)]
        xn2t = [sb(f"xn2t_{i}", [128, KC * 128], BF16) for i in range(2)]
        hw = [sb(f"hw{i}", [128, 128], BF16) for i in range(2)]
        hwt = [sb(f"hwt{i}", [128, 128], BF16) for i in range(2)]
        mask = sb("mask_sb", [128, 512], BF16)
        ident = sb("ident_sb", [128, 128], BF16)
        rb = sb("rb_sb", [128, 8], F32)
        stats = [sb(f"stats{i}", [128, 8], F32) for i in range(4)]
        pmx = sb("pmx", [128, 4], F32)
        nmx = sb("nmx", [128, 4], F32)
        psum_s = sb("psum_s", [128, 4], F32)
        rsum = sb("rsum", [128, 1], F32)
        recg = sb("recg", [128, 4], F32)
        epst = sb("epst", [128, 1], F32)
        eps6 = sb("eps6", [128, 1], F32)
        lg = [sb(f"lg{i}", [128, 8], F32) for i in range(2)]
        ge6 = [sb(f"ge6_{i}", [128, 8], F32) for i in range(2)]
        se6 = [sb(f"se6_{i}", [128, 1], F32) for i in range(2)]
        top8 = [sb(f"top8_{i}", [128, 8], F32) for i in range(2)]
        den = [sb(f"den{i}", [128, 1], F32) for i in range(2)]
        w6 = [sb(f"w6_{i}", [128, 8], F32) for i in range(2)]
        g_nm = sb("g_nm", [128, 1], F32)
        h_sb = [sb(f"h_sb{i}", [128, 118], F32) for i in range(2)]

        big = psalloc("ps_big", [128, 2048])
        aux = psalloc("ps_aux", [128, 1024])
        psh = psalloc("ps_h", [128, 128])
        tpa = psalloc("ps_tpa", [128, 128], BF16)

        try:
            rem = nc.sbuf_bytes_remaining
            print(f"[kernel build] sbuf bytes remaining/partition: {rem}")
        except Exception:
            pass

        sems = {}
        for s in ("sx", "sw", "so", "sd", "sd2", "st", "sv", "ss", "sg"):
            sems[s] = st_.enter_context(nc.semaphore(s))

        free = {}

        def waitf(key):
            return list(free.get(key, ()))

        def setf(key, *deps):
            free[key] = [d for d in deps if d is not None]

        def addf(key, *deps):
            free.setdefault(key, [])
            free[key].extend(d for d in deps if d is not None)

        def dma(eng, dst, src, waits=(), sem="sx"):
            return em.op(eng, lambda e, d=dst, s=src: e.dma_start(out=d, in_=s),
                         waits=waits, inc=sem, inc_by=16)

        def dmat(dst, src, waits=(), eng="sync"):
            sem = "sd" if eng == "sync" else "sd2"
            return em.op(eng,
                         lambda e, d=dst, s=src: e.dma_start_transpose(d, s),
                         waits=waits, inc=sem, inc_by=16)

        def vop(fn, waits=(), inc="sv"):
            return em.op("vector", fn, waits=waits, inc=inc)

        def sop(fn, waits=(), inc="ss"):
            return em.op("scalar", fn, waits=waits, inc=inc)

        def gop(fn, waits=(), inc="sg"):
            return em.op("gpsimd", fn, waits=waits, inc=inc)

        def top(fn, waits=(), inc="st"):
            return em.op("tensor", fn, waits=waits, inc=inc)

        c_mask = dma("sync", mask[:], mask_d[:])
        c_id = dma("sync", ident[:], id_d[:])
        c_rb = dma("sync", rb[:], rb_d[:])

        def wload(dst, src_d, waits=()):
            return dma("gpsimd", dst.rearrange("p (k n) -> p k n", k=KC),
                       src_d.rearrange("(k p) n -> p k n", p=128), waits=waits,
                       sem="sw")

        def wk_load(waits=()):
            dma("gpsimd", qmix.rearrange("p (k n) -> p k n", k=4),
                wk_d[0:512, :].rearrange("(k p) n -> p k n", p=128),
                waits=list(waits), sem="sw")
            return dma("gpsimd", ctx.rearrange("p (k n) -> p k n", k=4),
                       wk_d[512:1024, :].rearrange("(k p) n -> p k n", p=128),
                       waits=list(waits), sem="sw")

        def wk_sl(k, m):
            buf = qmix if k < 4 else ctx
            return buf[:, (k % 4) * D + m * 128: (k % 4) * D + (m + 1) * 128]

        w_kb = wk_load()
        w_base = wload(wsb_kb, wb_d)
        w_q = wload(wsb_q, wq_d)
        w_v = wload(wsb_v, wv_d)
        w_o = wload(wsb_o, wo_d)
        w_ac = wload(acat, acat_d)
        w_bc = dma("gpsimd", bcat[:], bcat_d[:], sem="sw")

        vop(lambda e: e.memset(epst[:], 1e-5))
        vop(lambda e: e.memset(eps6[:], 1e-6))
        for i in range(2):
            vop(lambda e, i=i: e.memset(ge6[i][:, 6:8], -1e30))
        for i in range(2):
            vop(lambda e, i=i: e.memset(hw[i][:, 112:113], 1.0))
            vop(lambda e, i=i: e.memset(hw[i][:, 113:128], 0.0))

        def bigs(i):
            return big[:, i * 512:(i + 1) * 512]

        def ln_norm(src, dst, par, xwait, dst_wait=(), sum_eng="vector"):
            stt = stats[par]
            sq_done = sop(lambda e, s=src: e.activation(
                out=trash[:, 0:D], in_=s, func=AF.Square, accum_out=stt[:, 1:2]),
                waits=list(xwait) + waitf(("stats", par)))
            if sum_eng == "scalar":
                sum_done = sop(lambda e, s=src: e.activation(
                    out=trash[:, 0:D], in_=s, func=AF.Copy,
                    accum_out=stt[:, 0:1]))
                sdep = ("ss", sum_done)
            else:
                sum_done = vop(lambda e, s=src: e.reduce_sum(
                    stt[:, 0:1], s, axis=AX.X), waits=list(xwait))
                sdep = ("sv", sum_done)
            m_done = vop(lambda e: e.tensor_scalar(
                out=stt[:, 2:3], in0=stt[:, 0:1], scalar1=1.0 / D, scalar2=None,
                op0=ALU.mult), waits=[sdep])
            nv_done = vop(lambda e: e.scalar_tensor_tensor(
                out=stt[:, 4:5], in0=stt[:, 2:3], scalar=stt[:, 0:1],
                in1=stt[:, 1:2], op0=ALU.mult, op1=ALU.subtract),
                waits=[("sv", m_done), ("ss", sq_done)])
            ln_done = sop(lambda e: e.activation(
                out=stt[:, 5:6], in_=stt[:, 4:5], func=AF.Ln, bias=epst[:],
                scale=-1.0 / D), waits=[("sv", nv_done)])
            rstd_done = sop(lambda e: e.activation(
                out=stt[:, 6:7], in_=stt[:, 5:6], func=AF.Exp, scale=-0.5))
            norm_done = vop(lambda e, s=src, d=dst: e.tensor_scalar(
                out=d, in0=s, scalar1=stt[:, 2:3], scalar2=stt[:, 6:7],
                op0=ALU.subtract, op1=ALU.mult),
                waits=[("ss", rstd_done)] + list(dst_wait))
            setf(("stats", par), ("sv", norm_done))
            return ("sv", norm_done)

        def hw_transpose(slot, src_ready):
            t_done = top(lambda e, s=slot: e.transpose(
                tpa[:], hw[s][:], ident[:]),
                waits=list(src_ready) + waitf(("tpa",)) + [("sx", c_id)])
            ev = vop(lambda e, s=slot: e.tensor_copy(hwt[s][:], tpa[:]),
                     waits=[("st", t_done)])
            setf(("tpa",), ("sv", ev))
            return ("sv", ev)

        def mm_group(out_ap, pieces, waits, evac_fn, evac_eng="scalar",
                     psum_key=None):
            n = len(pieces)
            stp = None
            for i, (l, r) in enumerate(pieces):
                w = list(waits) + (waitf(psum_key) if i == 0 else [])
                stp = top(lambda e, l=l, r=r, i=i, o=out_ap: e.matmul(
                    o, l, r, start=(i == 0), stop=(i == n - 1)),
                    waits=w if i == 0 else [], inc="st" if i == n - 1 else None)
            if evac_eng == "scalar":
                ev = sop(lambda e: evac_fn(e), waits=[("st", stp)])
                dep = ("ss", ev)
            else:
                ev = vop(lambda e: evac_fn(e), waits=[("st", stp)])
                dep = ("sv", ev)
            if psum_key is not None:
                setf(psum_key, dep)
            return dep, stp

        nstore = [0]
        base_last = [0]

        def ffn_head_dma(fc):
            sc = fc["sc"]
            ts = sc % 2
            tdp = dmat(
                xn2t[ts].rearrange("p (k s) -> p k s", s=128),
                xn2[sc].rearrange("p (k d) -> p k d", d=128),
                waits=[fc["nd2"]] + waitf(("xn2t", ts)))
            free.pop(("xn2t", ts), None)
            setf(("xn2", sc), ("sd", tdp))
            fc["t2"] = [("sd", tdp)]
            fc["ts"] = ts

        def ffn_head(fc):
            sc, slot = fc["sc"], fc["slot"]
            ts = fc["ts"]
            t2 = fc["t2"]
            h_stp = None
            for k in range(KC):
                w = (t2 + [("sw", w_ac)] + waitf(("psh",))) if k == 0 else []
                h_stp = top(lambda e, k=k, ts=ts: e.matmul(
                    psh[:, 0:120], xn2t[ts][:, k * 128:(k + 1) * 128],
                    acat[:, k * 120:(k + 1) * 120],
                    start=(k == 0), stop=(k == KC - 1)),
                    waits=w, inc="st" if k == KC - 1 else None)
            fc["h_stp"] = h_stp
            hp = sc % 2
            hev = sop(lambda e, hp=hp: e.copy(h_sb[hp][:], psh[:, 0:118]),
                      waits=[("st", h_stp)] + waitf(("h_sb", hp)))
            setf(("psh",), ("ss", hev))
            p = sc % 2
            d1 = vop(lambda e, p=p: e.tensor_tensor(
                lg[p][:, 0:6], h_sb[p][:, 112:118], rb[:, 0:6], ALU.add),
                waits=[("ss", hev), ("sx", c_rb)] + waitf(("lg", p)))
            d3 = sop(lambda e, p=p: e.activation(
                out=ge6[p][:, 0:6], in_=lg[p][:, 0:6], func=AF.Exp,
                accum_out=se6[p][:]), waits=[("sv", d1)])
            setf(("lg", p), ("ss", d3))
            d6 = vop(lambda e, p=p: e.max(out=top8[p][:], in_=ge6[p][:]),
                     waits=[("ss", d3)])
            d7 = vop(lambda e, p=p: e.scalar_tensor_tensor(
                out=ge6[p][:, 0:6], in0=ge6[p][:, 0:6], scalar=top8[p][:, 1:2],
                in1=ge6[p][:, 0:6], op0=ALU.is_ge, op1=ALU.mult),
                waits=[("sv", d6)])
            d8 = vop(lambda e, p=p: e.scalar_tensor_tensor(
                out=den[p][:], in0=se6[p][:], scalar=1e-6, in1=top8[p][:, 0:1],
                op0=ALU.mult, op1=ALU.add))
            d8b = vop(lambda e, p=p: e.tensor_scalar(
                out=den[p][:], in0=den[p][:], scalar1=top8[p][:, 1:2],
                scalar2=None, op0=ALU.add))
            d9 = vop(lambda e, p=p: e.reciprocal(den[p][:], den[p][:]),
                     waits=[("sv", d8b)])
            d10 = vop(lambda e, p=p: e.tensor_scalar(
                out=w6[p][:, 0:6], in0=ge6[p][:, 0:6], scalar1=den[p][:],
                scalar2=None, op0=ALU.mult), waits=[("sv", d9), ("sv", d7)])
            hb_last = vop(lambda e, slot=slot, p=p: e.tensor_copy(
                hw[slot % 2][:, 0:16], h_sb[p][:, 0:16]),
                waits=waitf(("hw", slot % 2)))
            for ex in range(6):
                hb_last = vop(lambda e, ex=ex, slot=slot, p=p: e.tensor_scalar(
                    out=hw[slot % 2][:, 16 + 16 * ex: 32 + 16 * ex],
                    in0=h_sb[p][:, 16 + 16 * ex: 32 + 16 * ex],
                    scalar1=w6[p][:, ex:ex + 1], scalar2=None, op0=ALU.mult),
                    waits=[("sv", d10)] if ex == 0 else [])
            free.pop(("hw", slot % 2), None)
            setf(("h_sb", p), ("sv", hb_last))
            fc["hb_deps"] = [("sv", hb_last)]

        def ffn_tail_base(fc):
            ts = fc["ts"]
            stp = None
            for half in (0, 1):
                for k in range(KC):
                    w = []
                    if k == 0:
                        w = (fc["t2"] + [("sw", fc["w_base"])] +
                             waitf(("aux", half)))
                    last = (half == 1 and k == KC - 1)
                    stp = top(lambda e, k=k, half=half, ts=ts: e.matmul(
                        aux[:, half * 512: half * 512 + 512],
                        xn2t[ts][:, k * 128:(k + 1) * 128],
                        wsb_kb[:, k * D + half * 512: k * D + half * 512 + 512],
                        start=(k == 0), stop=False), waits=w,
                        inc="st" if last else None)
            setf(("xn2t", ts), ("st", stp))

        def ffn_hwt(fc):
            slot = fc["slot"]
            fc["hwt_dep"] = hw_transpose(slot % 2, list(fc["hb_deps"]) +
                                         waitf(("hwt", slot % 2)))
            free.pop(("hwt", slot % 2), None)
            setf(("hw", slot % 2), fc["hwt_dep"])

        def ffn_tail_up(fc):
            seq, g, sc, slot, ts = fc["seq"], fc["g"], fc["sc"], fc["slot"], fc["ts"]
            ci = fc["ci"]
            row0 = fc["row0"]
            ffn_deps = []
            for half in (0, 1):
                stp = top(lambda e, half=half, ts=ts: e.matmul(
                    aux[:, half * 512: half * 512 + 512], hwt[slot % 2][:],
                    bcat[:, half * 512: half * 512 + 512],
                    start=False, stop=True),
                    waits=[fc["hwt_dep"], ("sw", w_bc)], inc="st")
                base_last[0] = stp
                evd = vop(lambda e, half=half, slot=slot: e.tensor_add(
                    x1b[slot][:, half * 512: half * 512 + 512],
                    x1b[slot][:, half * 512: half * 512 + 512],
                    aux[:, half * 512: half * 512 + 512]),
                    waits=[("st", stp), fc["nd2"]])
                ffn_deps.append(("sv", evd))
                setf(("aux", half), ("sv", evd))
            addf(("hwt", slot % 2), *ffn_deps)
            so_c = dma("sync", out_d[row0 + ci * 128: row0 + (ci + 1) * 128, :],
                       x1b[slot][:], waits=ffn_deps, sem="so")
            nstore[0] += 1
            setf(("x1", slot), ("so", so_c))

        boundary_nds = {}
        boundary_xl = {}
        for seq in range(NSEQ):
            row0 = seq * S
            ln1_done = [None] * NCH
            if seq > 0:
                w_kb = wk_load(waits=waitf(("qmix",)) + waitf(("ctx",)))

            kproj_last = 0
            xlv = {}

            def emit_loads(g):
                for c in range(4):
                    ci = g * 4 + c
                    slot = ci % 4
                    xlv[ci] = dma(
                        "sync", x1b[slot][:],
                        x_d[row0 + ci * 128: row0 + (ci + 1) * 128, :],
                        waits=waitf(("x1", slot)))

            def emit_lns(g):
                out = []
                for c in range(4):
                    ci = g * 4 + c
                    slot = ci % 4
                    nd = ln_norm(x1b[slot][:], xn1[:, ci * D:(ci + 1) * D],
                                 ci % 4, [("sx", xlv[ci])],
                                 dst_wait=waitf(("xn1",)))
                    ln1_done[ci] = nd
                    setf(("x1", slot), nd)
                    out.append(nd)
                return out

            if boundary_nds:
                nds = [boundary_nds[c] for c in range(4)]
                for c in range(4):
                    ln1_done[c] = nds[c]
                boundary_nds = {}
                boundary_xl = {}
            else:
                emit_loads(0)
                nds = emit_lns(0)
            for g in range(G):
                if g + 1 < G:
                    emit_loads(g + 1)
                td = []
                for c in range(4):
                    ci = g * 4 + c
                    tdp = dmat(
                        xnt.rearrange("p (k s) -> p k s", s=512)[:, :, c * 128:(c + 1) * 128],
                        xn1.rearrange("p (c k d) -> p c k d", c=NCH, k=KC)[:, ci],
                        waits=[nds[c]] + waitf(("xnt",)))
                    td.append(("sd", tdp))
                if g + 1 < G:
                    nds = emit_lns(g + 1)
                for m in range(KC):
                    sl = m % 4
                    pieces = [(wk_sl(k, m),
                               xnt[:, k * 512:(k + 1) * 512]) for k in range(KC)]
                    dst = kt[:, m * S + g * 512: m * S + (g + 1) * 512]
                    dep, stp = mm_group(
                        bigs(sl), pieces, waits=td + [("sw", w_kb)],
                        evac_fn=(lambda e, d=dst, sl=sl: e.copy(d, bigs(sl))),
                        psum_key=("big", sl))
                    kproj_last = stp
                setf(("xnt",), ("st", kproj_last))
            k_evac_ss = em.now("ss")
            setf(("qmix",), ("st", kproj_last))
            setf(("ctx",), ("st", kproj_last))

            def emit_qtp(g):
                td = []
                for c in range(4):
                    ci = g * 4 + c
                    tdp = dmat(
                        xnt.rearrange("p (k s) -> p k s", s=512)[:, :, c * 128:(c + 1) * 128],
                        xn1.rearrange("p (c k d) -> p c k d", c=NCH, k=KC)[:, ci],
                        waits=[ln1_done[ci]] + waitf(("xnt",)))
                    td.append(("sd", tdp))
                return td

            pending = []
            td_next = emit_qtp(0)
            for g in range(G):
                td = td_next
                q_deps = []
                q_last = 0
                for m in range(KC):
                    sl = m % 4
                    pieces = [(wsb_q[:, k * D + m * 128: k * D + (m + 1) * 128],
                               xnt[:, k * 512:(k + 1) * 512]) for k in range(KC)]
                    dst = qmix[:, m * 512:(m + 1) * 512]
                    dep, stp = mm_group(
                        bigs(sl), pieces,
                        waits=td + [("sw", w_q)] + waitf(("qmix",)),
                        evac_fn=(lambda e, d=dst, sl=sl: e.activation(
                            out=d, in_=bigs(sl), func=AF.Copy, scale=1.0 / 32.0)),
                        psum_key=("big", sl))
                    q_deps.append(dep)
                    q_last = stp
                free.pop(("qmix",), None)
                setf(("xnt",), ("st", q_last))
                if g + 1 < G:
                    td_next = emit_qtp(g + 1)
                pz = vop(lambda e, g=g: e.memset(
                    ptg[:, g * 4 * 512:(g * 4 + 4) * 512], 0.0),
                    waits=waitf(("ptg",)))
                free.pop(("ptg",), None)

                pt_sd = []
                exp_all = 0
                for qt in range(4):
                    qg = g * 4 + qt
                    tmax = (qg + 1) * 128
                    nt = ceil_div(tmax, 512)
                    if nt <= 2:
                        boff = 2 * (qt % 2)
                    elif nt == 3:
                        boff = 3 * (qt % 2)
                    else:
                        boff = 0
                    stps = []
                    pm_deps = []
                    if nt >= 3:
                        tc_order = [0, nt - 1] + list(range(1, nt - 1))
                    else:
                        tc_order = list(range(nt))
                    for tc in tc_order:
                        w = min(512, tmax - tc * 512)
                        stp = None
                        for k in range(KC):
                            ww = []
                            if k == 0:
                                ww = waitf(("big", (boff + tc) % 4))
                                if tc == 0:
                                    ww = ww + [("ss", k_evac_ss)] + \
                                        [(s, v) for (s, v) in q_deps]
                            stp = top(lambda e, k=k, tc=tc, w=w, qt=qt, boff=boff: e.matmul(
                                big[:, ((boff + tc) % 4) * 512: ((boff + tc) % 4) * 512 + w],
                                qmix[:, k * 512 + qt * 128: k * 512 + (qt + 1) * 128],
                                kt[:, k * S + tc * 512: k * S + tc * 512 + w],
                                start=(k == 0), stop=(k == KC - 1)),
                                waits=ww, inc="st" if k == KC - 1 else None)
                        stps.append(stp)
                        last_emitted = (tc == tc_order[-1])
                        if last_emitted and pending:
                            if qt >= 1:
                                ffn_tail_base(pending[qt - 1])
                                if qt + 1 < 4:
                                    ffn_head_dma(pending[qt + 1])
                        if tc == nt - 1:
                            off = qt * 128
                            mk = vop(lambda e, tc=tc, w=w, off=off, boff=boff: e.tensor_tensor(
                                big[:, ((boff + tc) % 4) * 512: ((boff + tc) % 4) * 512 + w],
                                big[:, ((boff + tc) % 4) * 512: ((boff + tc) % 4) * 512 + w],
                                mask[:, 384 - off: 384 - off + w], ALU.add),
                                waits=[("st", stp), ("sx", c_mask)])
                            pm = vop(lambda e, tc=tc, w=w, boff=boff: e.reduce_max(
                                pmx[:, tc:tc + 1],
                                big[:, ((boff + tc) % 4) * 512: ((boff + tc) % 4) * 512 + w],
                                axis=AX.X), waits=[("sv", mk)])
                        else:
                            pm = vop(lambda e, tc=tc, w=w, boff=boff: e.reduce_max(
                                pmx[:, tc:tc + 1],
                                big[:, ((boff + tc) % 4) * 512: ((boff + tc) % 4) * 512 + w],
                                axis=AX.X), waits=[("st", stp)] + waitf(("pmx",)))
                        pm_deps.append(("sv", pm))
                    nm_done = vop(lambda e, nt=nt, qt=qt: e.reduce_max(
                        nmx[:, qt:qt + 1], pmx[:, 0:nt], axis=AX.X, negate=True),
                        waits=pm_deps + waitf(("nmx", qt)))
                    free.pop(("nmx", qt), None)
                    setf(("pmx",), ("sv", nm_done))
                    e_done = 0
                    for tc in range(nt):
                        w = min(512, tmax - tc * 512)
                        e_done = sop(lambda e, tc=tc, w=w, qt=qt, boff=boff: e.activation(
                            out=pbf[:, tc * 512: tc * 512 + w],
                            in_=big[:, ((boff + tc) % 4) * 512: ((boff + tc) % 4) * 512 + w],
                            func=AF.Exp, bias=nmx[:, qt:qt + 1],
                            accum_out=psum_s[:, tc:tc + 1]),
                            waits=[("sv", nm_done)] + waitf(("pbf",)) +
                                  waitf(("psum_s",)))
                        setf(("big", (boff + tc) % 4), ("ss", e_done))
                    free.pop(("pbf",), None)
                    free.pop(("psum_s",), None)
                    exp_all = e_done
                    setf(("nmx", qt), ("ss", e_done))
                    if pending and qt >= 1:
                        ffn_hwt(pending[qt - 1])
                    rs = vop(lambda e, nt=nt: e.reduce_sum(
                        rsum[:], psum_s[:, 0:nt], axis=AX.X),
                        waits=[("ss", e_done)])
                    setf(("psum_s",), ("sv", rs))
                    rc = vop(lambda e, qt=qt: e.reciprocal(
                        recg[:, qt:qt + 1], rsum[:]),
                        waits=[("sv", rs)] + waitf(("recg", qt)))
                    free.pop(("recg", qt), None)
                    setf(("recg2", qt), ("sv", rc))
                    ntc2 = qg + 1
                    pt = dmat(
                        ptg.rearrange("p (c q) -> p c q", q=512)[:, 0:ntc2, qt * 128:(qt + 1) * 128],
                        pbf.rearrange("p (c t) -> p c t", t=128)[:, 0:ntc2, :],
                        waits=[("ss", e_done), ("sv", pz)] + waitf(("ptg_w",)))
                    pt_sd.append(pt)
                    setf(("pbf",), ("sd", pt))
                    if pending:
                        if qt >= 1:
                            ffn_tail_up(pending[qt - 1])
                        ffn_head(pending[qt])
                if pending:
                    ffn_tail_base(pending[3])
                    ffn_hwt(pending[3])
                    ffn_tail_up(pending[3])
                    pending = []

                xlb = {}
                for sc in range(4):
                    ci = g * 4 + sc
                    xlb[sc] = dma("sync", x1b[sc][:],
                                  x_d[row0 + ci * 128: row0 + (ci + 1) * 128, :],
                                  waits=waitf(("x1", sc)))
                ntc2 = (g + 1) * 4
                mix_deps = []
                mix_last = 0
                for dch in range(KC):
                    sl = dch % 4
                    pieces = [(xn1[:, tc2 * D + dch * 128: tc2 * D + (dch + 1) * 128],
                               ptg[:, tc2 * 512:(tc2 + 1) * 512])
                              for tc2 in range(ntc2)]
                    dst = qmix[:, dch * 512:(dch + 1) * 512]
                    dep, stp = mm_group(
                        bigs(sl), pieces,
                        waits=[("sd", pt_sd[-1]), ("ss", exp_all)],
                        evac_fn=(lambda e, d=dst, sl=sl: e.copy(d, bigs(sl))),
                        psum_key=("big", sl))
                    mix_deps.append(dep)
                    mix_last = stp
                setf(("ptg_w",), ("st", mix_last))
                setf(("ptg",), ("st", mix_last))
                ctx_deps = []
                ctx_last = 0
                for dv in range(KC):
                    sl = dv % 4
                    pieces = [(wsb_v[:, k * D + dv * 128: k * D + (dv + 1) * 128],
                               qmix[:, k * 512:(k + 1) * 512]) for k in range(KC)]
                    dst = ctx[:, dv * 512:(dv + 1) * 512]
                    dep, stp = mm_group(
                        bigs(sl), pieces,
                        waits=[(s, v) for (s, v) in mix_deps] + [("sw", w_v)] +
                              waitf(("ctx",)),
                        evac_fn=(lambda e, d=dst, sl=sl: e.copy(d, bigs(sl))),
                        psum_key=("big", sl))
                    ctx_deps.append(dep)
                    ctx_last = stp
                free.pop(("ctx",), None)
                setf(("qmix",), ("st", ctx_last))

                wo_last = 0
                for sc in range(4):
                    ci = g * 4 + sc
                    slot = sc
                    xl = xlb[sc]
                    at_deps = []
                    for half in (0, 1):
                        sl = (sc % 2) * 2 + half
                        pieces = [(ctx[:, dv * 512 + sc * 128: dv * 512 + (sc + 1) * 128],
                                   wsb_o[:, dv * D + half * 512: dv * D + half * 512 + 512])
                                  for dv in range(KC)]
                        def ev(e, half=half, sl=sl, slot=slot, sc=sc):
                            return e.scalar_tensor_tensor(
                                out=x1b[slot][:, half * 512: half * 512 + 512],
                                in0=bigs(sl), scalar=recg[:, sc:sc + 1],
                                in1=x1b[slot][:, half * 512: half * 512 + 512],
                                op0=ALU.mult, op1=ALU.add)
                        dep, stp = mm_group(
                            bigs(sl), pieces,
                            waits=[(s, v) for (s, v) in ctx_deps] +
                                  [("sw", w_o), ("sx", xl)] +
                                  waitf(("recg2", sc)),
                            evac_fn=ev, evac_eng="vector", psum_key=("big", sl))
                        at_deps.append(dep)
                        wo_last = stp
                    setf(("recg", sc), *at_deps)
                    nd2 = ln_norm(x1b[slot][:], xn2[sc][:], sc % 4,
                                  [(s, v) for (s, v) in at_deps],
                                  dst_wait=waitf(("xn2", sc)))
                    pending.append(dict(seq=seq, g=g, sc=sc, ci=ci, slot=slot,
                                        nd2=nd2, row0=row0, w_base=w_base))
                    if sc <= 1:
                        ffn_head_dma(pending[-1])
                    if g == G - 1 and sc >= 2:
                        ffn_head(pending[sc - 2])
                setf(("ctx",), ("st", wo_last))
                addf(("xn1",), ("st", mix_last))
                if g == G - 1:
                    ffn_tail_base(pending[0])
                    ffn_hwt(pending[0])
                    ffn_tail_up(pending[0])
                    ffn_head_dma(pending[2])
                    ffn_head(pending[2])
                    ffn_tail_base(pending[1])
                    ffn_hwt(pending[1])
                    ffn_tail_up(pending[1])
                    ffn_head_dma(pending[3])
                    ffn_head(pending[3])
                    ffn_tail_base(pending[2])
                    ffn_hwt(pending[2])
                    ffn_tail_up(pending[2])
                    ffn_tail_base(pending[3])
                    ffn_hwt(pending[3])
                    ffn_tail_up(pending[3])

            nrow0 = (seq + 1) * S

            def bnd(c):
                if seq + 1 >= NSEQ:
                    return
                boundary_xl[c] = dma(
                    "sync", x1b[c][:],
                    x_d[nrow0 + c * 128: nrow0 + (c + 1) * 128, :],
                    waits=waitf(("x1", c)))
                nd = ln_norm(x1b[c][:], xn1[:, c * D:(c + 1) * D], c % 4,
                             [("sx", boundary_xl[c])],
                             dst_wait=waitf(("xn1",)))
                boundary_nds[c] = nd
                setf(("x1", c), nd)

            bnd(0)
            bnd(1)
            bnd(2)
            bnd(3)
            pending = []

        em.op("sync", lambda e: None, waits=[("so", em.now("so"))])

        block = st_.enter_context(nc.Block())

        def runner(qname):
            def run(eng):
                for ws, fn, inc, ib in em.q[qname]:
                    for s, v in ws:
                        eng.wait_ge(sems[s], v)
                    ins = fn(eng)
                    if inc is not None and ins is not None:
                        ins.then_inc(sems[inc], ib)
            return run

        block.tensor(runner("tensor"))
        block.vector(runner("vector"))
        block.scalar(runner("scalar"))
        block.sync(runner("sync"))
        block.gpsimd(runner("gpsimd"))

    return nc


def _host_layout(inputs, S, NSEQ):
    x = np.ascontiguousarray(np.asarray(inputs["L0_attn_x"], dtype=np.float32))
    B = x.shape[0]
    ncore = B // NSEQ
    wq = np.ascontiguousarray(np.asarray(inputs["L0_attn_Wq"], np.float32))
    wk = np.ascontiguousarray(np.asarray(inputs["L0_attn_Wk"], np.float32))
    wv = np.ascontiguousarray(np.asarray(inputs["L0_attn_Wv"], np.float32))
    wo = np.ascontiguousarray(np.asarray(inputs["L0_attn_Wo"], np.float32))
    wbase = np.ascontiguousarray(np.asarray(inputs["L0_ffn_base_W"], np.float32).T)
    shared_A = np.asarray(inputs["L0_ffn_shared_A"], np.float32)
    shared_B = np.asarray(inputs["L0_ffn_shared_B"], np.float32)
    routed_A = np.asarray(inputs["L0_ffn_routed_A"], np.float32)
    routed_B = np.asarray(inputs["L0_ffn_routed_B"], np.float32)
    router_W = np.asarray(inputs["L0_ffn_router_W"], np.float32)
    router_b = np.asarray(inputs["L0_ffn_router_b"], np.float32)
    base_b = np.asarray(inputs["L0_ffn_base_b"], np.float32)
    E = routed_A.shape[0]

    acat = np.zeros((D, 120), np.float32)
    acat[:, 0:16] = shared_A.T
    for e in range(E):
        acat[:, 16 + 16 * e: 32 + 16 * e] = routed_A[e].T
    acat[:, 112:118] = router_W.T
    acat = np.ascontiguousarray(acat)

    bcat = np.zeros((128, D), np.float32)
    bcat[0:16, :] = shared_B.T
    for e in range(E):
        bcat[16 + 16 * e: 32 + 16 * e, :] = routed_B[e].T
    bcat[112, :] = base_b
    bcat = np.ascontiguousarray(bcat)

    rb = np.zeros((128, 8), np.float32)
    rb[:, 0:6] = router_b[None, :]

    mask = np.zeros((128, 512), np.float32)
    ss, cc = np.meshgrid(np.arange(128), np.arange(512), indexing="ij")
    mask[cc > ss + 384] = MASKV
    mask = mask.astype(BF)

    ident = np.eye(128, dtype=BF)
    shared = {"wq": wq, "wk": wk, "wv": wv, "wo": wo, "wbase": wbase,
              "acat": acat, "bcat": bcat, "rb": rb, "mask": mask,
              "ident": ident}
    in_maps = []
    for c in range(ncore):
        m = dict(shared)
        m["x"] = np.ascontiguousarray(
            x[c * NSEQ:(c + 1) * NSEQ].reshape(NSEQ * S, D))
        in_maps.append(m)
    return in_maps, ncore


def kernel(**inputs):
    S = 2048
    NSEQ = 2
    in_maps, ncore = _host_layout(inputs, S, NSEQ)
    nc = build_program(S, NSEQ)
    res = run_bass_kernel_spmd(nc, in_maps, core_ids=list(range(ncore)))
    outs = [np.asarray(res.results[c]["out"]).reshape(NSEQ, S, D)
            for c in range(ncore)]
    return np.concatenate(outs, axis=0)



# version 39
# speedup vs baseline: 1.0414x; 1.0005x over previous
import math
import numpy as np
import ml_dtypes
import concourse.bass as bass
import concourse.mybir as mybir
from concourse.bass_utils import run_bass_kernel_spmd

F32 = mybir.dt.float32
BF16 = mybir.dt.bfloat16
AF = mybir.ActivationFunctionType
ALU = mybir.AluOpType
AX = mybir.AxisListType
BF = ml_dtypes.bfloat16

D = 1024
KC = 8
MASKV = -60.0

ENGINES = ("tensor", "vector", "scalar", "sync", "gpsimd")


class Em:
    def __init__(self):
        self.q = {e: [] for e in ENGINES}
        self.cnt = {}
        self.lastwait = {e: {} for e in ENGINES}

    def now(self, sem):
        return self.cnt.get(sem, 0)

    def op(self, eng, fn, waits=(), inc=None, inc_by=1):
        lw = self.lastwait[eng]
        ws = []
        for s, v in waits:
            if v is None or v <= 0:
                continue
            if v > lw.get(s, 0):
                lw[s] = v
                ws.append((s, v))
        if inc is not None:
            self.cnt[inc] = self.now(inc) + inc_by
        self.q[eng].append((tuple(ws), fn, inc, inc_by))
        return self.now(inc) if inc is not None else None


def ceil_div(a, b):
    return -(-a // b)


def build_program(S, NSEQ):
    G = S // 512
    NCH = S // 128
    NTC = S // 128
    nc = bass.Bass("TRN2", target_bir_lowering=False)

    x_d = nc.dram_tensor("x", [NSEQ * S, D], F32, kind="ExternalInput")
    wq_d = nc.dram_tensor("wq", [D, D], F32, kind="ExternalInput")
    wk_d = nc.dram_tensor("wk", [D, D], F32, kind="ExternalInput")
    wv_d = nc.dram_tensor("wv", [D, D], F32, kind="ExternalInput")
    wo_d = nc.dram_tensor("wo", [D, D], F32, kind="ExternalInput")
    wb_d = nc.dram_tensor("wbase", [D, D], F32, kind="ExternalInput")
    acat_d = nc.dram_tensor("acat", [D, 120], F32, kind="ExternalInput")
    bcat_d = nc.dram_tensor("bcat", [128, D], F32, kind="ExternalInput")
    rb_d = nc.dram_tensor("rb", [128, 8], F32, kind="ExternalInput")
    mask_d = nc.dram_tensor("mask", [128, 512], BF16, kind="ExternalInput")
    id_d = nc.dram_tensor("ident", [128, 128], BF16, kind="ExternalInput")
    out_d = nc.dram_tensor("out", [NSEQ * S, D], F32, kind="ExternalOutput")

    em = Em()
    from contextlib import ExitStack
    with ExitStack() as st_:
        def sb(name, shape, dtype):
            return st_.enter_context(nc.sbuf_tensor(name, shape, dtype))

        def psalloc(name, shape, dtype=F32):
            return st_.enter_context(nc.psum_tensor(name, shape, dtype))

        wsb_q = sb("wsb_q", [128, KC * D], BF16)
        wsb_kb = sb("wsb_kb", [128, KC * D], BF16)
        wsb_v = sb("wsb_v", [128, KC * D], BF16)
        wsb_o = sb("wsb_o", [128, KC * D], BF16)
        acat = sb("acat_sb", [128, KC * 120], BF16)
        bcat = sb("bcat_sb", [128, D], BF16)
        xn1 = sb("xn1_tm", [128, NCH * D], BF16)
        kt = sb("kt_fm", [128, KC * S], BF16)
        xnt = sb("xnt_g", [128, KC * 512], BF16)
        qmix = sb("qmix_g", [128, KC * 512], BF16)
        ctx = sb("ctx_g", [128, KC * 512], BF16)
        ptg = sb("pt_g", [128, NTC * 512], BF16)
        pbf = sb("p_bf", [128, S], BF16)
        trash = pbf
        x1b = [sb(f"x1_{i}", [128, D], F32) for i in range(4)]
        xn2 = [sb(f"xn2_{i}", [128, D], BF16) for i in range(4)]
        xn2t = [sb(f"xn2t_{i}", [128, KC * 128], BF16) for i in range(2)]
        hw = [sb(f"hw{i}", [128, 128], BF16) for i in range(2)]
        hwt = [sb(f"hwt{i}", [128, 128], BF16) for i in range(2)]
        mask = sb("mask_sb", [128, 512], BF16)
        ident = sb("ident_sb", [128, 128], BF16)
        rb = sb("rb_sb", [128, 8], F32)
        stats = [sb(f"stats{i}", [128, 8], F32) for i in range(4)]
        pmx = sb("pmx", [128, 4], F32)
        nmx = sb("nmx", [128, 4], F32)
        psum_s = sb("psum_s", [128, 4], F32)
        rsum = sb("rsum", [128, 1], F32)
        recg = sb("recg", [128, 4], F32)
        epst = sb("epst", [128, 1], F32)
        eps6 = sb("eps6", [128, 1], F32)
        lg = [sb(f"lg{i}", [128, 8], F32) for i in range(2)]
        ge6 = [sb(f"ge6_{i}", [128, 8], F32) for i in range(2)]
        se6 = [sb(f"se6_{i}", [128, 1], F32) for i in range(2)]
        top8 = [sb(f"top8_{i}", [128, 8], F32) for i in range(2)]
        den = [sb(f"den{i}", [128, 1], F32) for i in range(2)]
        w6 = [sb(f"w6_{i}", [128, 8], F32) for i in range(2)]
        g_nm = sb("g_nm", [128, 1], F32)
        h_sb = [sb(f"h_sb{i}", [128, 118], F32) for i in range(2)]

        big = psalloc("ps_big", [128, 2048])
        aux = psalloc("ps_aux", [128, 1024])
        psh = psalloc("ps_h", [128, 128])
        tpa = psalloc("ps_tpa", [128, 128], BF16)

        try:
            rem = nc.sbuf_bytes_remaining
            print(f"[kernel build] sbuf bytes remaining/partition: {rem}")
        except Exception:
            pass

        sems = {}
        for s in ("sx", "sw", "so", "sd", "sd2", "st", "sv", "ss", "sg"):
            sems[s] = st_.enter_context(nc.semaphore(s))

        free = {}

        def waitf(key):
            return list(free.get(key, ()))

        def setf(key, *deps):
            free[key] = [d for d in deps if d is not None]

        def addf(key, *deps):
            free.setdefault(key, [])
            free[key].extend(d for d in deps if d is not None)

        def dma(eng, dst, src, waits=(), sem="sx"):
            return em.op(eng, lambda e, d=dst, s=src: e.dma_start(out=d, in_=s),
                         waits=waits, inc=sem, inc_by=16)

        def dmat(dst, src, waits=(), eng="sync"):
            sem = "sd" if eng == "sync" else "sd2"
            return em.op(eng,
                         lambda e, d=dst, s=src: e.dma_start_transpose(d, s),
                         waits=waits, inc=sem, inc_by=16)

        def vop(fn, waits=(), inc="sv"):
            return em.op("vector", fn, waits=waits, inc=inc)

        def sop(fn, waits=(), inc="ss"):
            return em.op("scalar", fn, waits=waits, inc=inc)

        def gop(fn, waits=(), inc="sg"):
            return em.op("gpsimd", fn, waits=waits, inc=inc)

        def top(fn, waits=(), inc="st"):
            return em.op("tensor", fn, waits=waits, inc=inc)

        c_mask = dma("sync", mask[:], mask_d[:])
        c_id = dma("sync", ident[:], id_d[:])
        c_rb = dma("sync", rb[:], rb_d[:])

        def wload(dst, src_d, waits=()):
            return dma("gpsimd", dst.rearrange("p (k n) -> p k n", k=KC),
                       src_d.rearrange("(k p) n -> p k n", p=128), waits=waits,
                       sem="sw")

        def wk_load(waits=()):
            dma("gpsimd", qmix.rearrange("p (k n) -> p k n", k=4),
                wk_d[0:512, :].rearrange("(k p) n -> p k n", p=128),
                waits=list(waits), sem="sw")
            return dma("gpsimd", ctx.rearrange("p (k n) -> p k n", k=4),
                       wk_d[512:1024, :].rearrange("(k p) n -> p k n", p=128),
                       waits=list(waits), sem="sw")

        def wk_sl(k, m):
            buf = qmix if k < 4 else ctx
            return buf[:, (k % 4) * D + m * 128: (k % 4) * D + (m + 1) * 128]

        w_kb = wk_load()
        w_base = wload(wsb_kb, wb_d)
        w_q = wload(wsb_q, wq_d)
        w_v = wload(wsb_v, wv_d)
        w_o = wload(wsb_o, wo_d)
        w_ac = wload(acat, acat_d)
        w_bc = dma("gpsimd", bcat[:], bcat_d[:], sem="sw")

        vop(lambda e: e.memset(epst[:], 1e-5))
        vop(lambda e: e.memset(eps6[:], 1e-6))
        for i in range(2):
            vop(lambda e, i=i: e.memset(ge6[i][:, 6:8], -1e30))
        for i in range(2):
            vop(lambda e, i=i: e.memset(hw[i][:, 112:113], 1.0))
            vop(lambda e, i=i: e.memset(hw[i][:, 113:128], 0.0))

        def bigs(i):
            return big[:, i * 512:(i + 1) * 512]

        def ln_norm(src, dst, par, xwait, dst_wait=(), sum_eng="vector"):
            stt = stats[par]
            sq_done = sop(lambda e, s=src: e.activation(
                out=trash[:, 0:D], in_=s, func=AF.Square, accum_out=stt[:, 1:2]),
                waits=list(xwait) + waitf(("stats", par)))
            if sum_eng == "scalar":
                sum_done = sop(lambda e, s=src: e.activation(
                    out=trash[:, 0:D], in_=s, func=AF.Copy,
                    accum_out=stt[:, 0:1]))
                sdep = ("ss", sum_done)
            else:
                sum_done = vop(lambda e, s=src: e.reduce_sum(
                    stt[:, 0:1], s, axis=AX.X), waits=list(xwait))
                sdep = ("sv", sum_done)
            m_done = vop(lambda e: e.tensor_scalar(
                out=stt[:, 2:3], in0=stt[:, 0:1], scalar1=1.0 / D, scalar2=None,
                op0=ALU.mult), waits=[sdep])
            nv_done = vop(lambda e: e.scalar_tensor_tensor(
                out=stt[:, 4:5], in0=stt[:, 2:3], scalar=stt[:, 0:1],
                in1=stt[:, 1:2], op0=ALU.mult, op1=ALU.subtract),
                waits=[("sv", m_done), ("ss", sq_done)])
            ln_done = sop(lambda e: e.activation(
                out=stt[:, 5:6], in_=stt[:, 4:5], func=AF.Ln, bias=epst[:],
                scale=-1.0 / D), waits=[("sv", nv_done)])
            rstd_done = sop(lambda e: e.activation(
                out=stt[:, 6:7], in_=stt[:, 5:6], func=AF.Exp, scale=-0.5))
            norm_done = vop(lambda e, s=src, d=dst: e.tensor_scalar(
                out=d, in0=s, scalar1=stt[:, 2:3], scalar2=stt[:, 6:7],
                op0=ALU.subtract, op1=ALU.mult),
                waits=[("ss", rstd_done)] + list(dst_wait))
            setf(("stats", par), ("sv", norm_done))
            return ("sv", norm_done)

        def hw_transpose(slot, src_ready):
            t_done = top(lambda e, s=slot: e.transpose(
                tpa[:], hw[s][:], ident[:]),
                waits=list(src_ready) + waitf(("tpa",)) + [("sx", c_id)])
            ev = vop(lambda e, s=slot: e.tensor_copy(hwt[s][:], tpa[:]),
                     waits=[("st", t_done)])
            setf(("tpa",), ("sv", ev))
            return ("sv", ev)

        def mm_group(out_ap, pieces, waits, evac_fn, evac_eng="scalar",
                     psum_key=None):
            n = len(pieces)
            stp = None
            for i, (l, r) in enumerate(pieces):
                w = list(waits) + (waitf(psum_key) if i == 0 else [])
                stp = top(lambda e, l=l, r=r, i=i, o=out_ap: e.matmul(
                    o, l, r, start=(i == 0), stop=(i == n - 1)),
                    waits=w if i == 0 else [], inc="st" if i == n - 1 else None)
            if evac_eng == "scalar":
                ev = sop(lambda e: evac_fn(e), waits=[("st", stp)])
                dep = ("ss", ev)
            else:
                ev = vop(lambda e: evac_fn(e), waits=[("st", stp)])
                dep = ("sv", ev)
            if psum_key is not None:
                setf(psum_key, dep)
            return dep, stp

        nstore = [0]
        base_last = [0]

        def ffn_head_dma(fc):
            sc = fc["sc"]
            ts = sc % 2
            tdp = dmat(
                xn2t[ts].rearrange("p (k s) -> p k s", s=128),
                xn2[sc].rearrange("p (k d) -> p k d", d=128),
                waits=[fc["nd2"]] + waitf(("xn2t", ts)))
            free.pop(("xn2t", ts), None)
            setf(("xn2", sc), ("sd", tdp))
            fc["t2"] = [("sd", tdp)]
            fc["ts"] = ts

        def ffn_head(fc):
            sc, slot = fc["sc"], fc["slot"]
            ts = fc["ts"]
            t2 = fc["t2"]
            h_stp = None
            for k in range(KC):
                w = (t2 + [("sw", w_ac)] + waitf(("psh",))) if k == 0 else []
                h_stp = top(lambda e, k=k, ts=ts: e.matmul(
                    psh[:, 0:120], xn2t[ts][:, k * 128:(k + 1) * 128],
                    acat[:, k * 120:(k + 1) * 120],
                    start=(k == 0), stop=(k == KC - 1)),
                    waits=w, inc="st" if k == KC - 1 else None)
            fc["h_stp"] = h_stp
            hp = sc % 2
            hev = sop(lambda e, hp=hp: e.copy(h_sb[hp][:], psh[:, 0:118]),
                      waits=[("st", h_stp)] + waitf(("h_sb", hp)))
            setf(("psh",), ("ss", hev))
            p = sc % 2
            d1 = vop(lambda e, p=p: e.tensor_tensor(
                lg[p][:, 0:6], h_sb[p][:, 112:118], rb[:, 0:6], ALU.add),
                waits=[("ss", hev), ("sx", c_rb)] + waitf(("lg", p)))
            d3 = sop(lambda e, p=p: e.activation(
                out=ge6[p][:, 0:6], in_=lg[p][:, 0:6], func=AF.Exp,
                accum_out=se6[p][:]), waits=[("sv", d1)])
            setf(("lg", p), ("ss", d3))
            d6 = vop(lambda e, p=p: e.max(out=top8[p][:], in_=ge6[p][:]),
                     waits=[("ss", d3)])
            d7 = vop(lambda e, p=p: e.scalar_tensor_tensor(
                out=ge6[p][:, 0:6], in0=ge6[p][:, 0:6], scalar=top8[p][:, 1:2],
                in1=ge6[p][:, 0:6], op0=ALU.is_ge, op1=ALU.mult),
                waits=[("sv", d6)])
            d8 = vop(lambda e, p=p: e.scalar_tensor_tensor(
                out=den[p][:], in0=se6[p][:], scalar=1e-6, in1=top8[p][:, 0:1],
                op0=ALU.mult, op1=ALU.add))
            d8b = vop(lambda e, p=p: e.tensor_scalar(
                out=den[p][:], in0=den[p][:], scalar1=top8[p][:, 1:2],
                scalar2=None, op0=ALU.add))
            d9 = vop(lambda e, p=p: e.reciprocal(den[p][:], den[p][:]),
                     waits=[("sv", d8b)])
            d10 = vop(lambda e, p=p: e.tensor_scalar(
                out=w6[p][:, 0:6], in0=ge6[p][:, 0:6], scalar1=den[p][:],
                scalar2=None, op0=ALU.mult), waits=[("sv", d9), ("sv", d7)])
            hb_last = vop(lambda e, slot=slot, p=p: e.tensor_copy(
                hw[slot % 2][:, 0:16], h_sb[p][:, 0:16]),
                waits=waitf(("hw", slot % 2)))
            for ex in range(6):
                hb_last = vop(lambda e, ex=ex, slot=slot, p=p: e.tensor_scalar(
                    out=hw[slot % 2][:, 16 + 16 * ex: 32 + 16 * ex],
                    in0=h_sb[p][:, 16 + 16 * ex: 32 + 16 * ex],
                    scalar1=w6[p][:, ex:ex + 1], scalar2=None, op0=ALU.mult),
                    waits=[("sv", d10)] if ex == 0 else [])
            free.pop(("hw", slot % 2), None)
            setf(("h_sb", p), ("sv", hb_last))
            fc["hb_deps"] = [("sv", hb_last)]

        def ffn_tail_base(fc):
            ts = fc["ts"]
            stp = None
            for half in (0, 1):
                for k in range(KC):
                    w = []
                    if k == 0:
                        w = (fc["t2"] + [("sw", fc["w_base"])] +
                             waitf(("aux", half)))
                    last = (half == 1 and k == KC - 1)
                    stp = top(lambda e, k=k, half=half, ts=ts: e.matmul(
                        aux[:, half * 512: half * 512 + 512],
                        xn2t[ts][:, k * 128:(k + 1) * 128],
                        wsb_kb[:, k * D + half * 512: k * D + half * 512 + 512],
                        start=(k == 0), stop=False), waits=w,
                        inc="st" if last else None)
            setf(("xn2t", ts), ("st", stp))

        def ffn_hwt(fc):
            slot = fc["slot"]
            fc["hwt_dep"] = hw_transpose(slot % 2, list(fc["hb_deps"]) +
                                         waitf(("hwt", slot % 2)))
            free.pop(("hwt", slot % 2), None)
            setf(("hw", slot % 2), fc["hwt_dep"])

        def ffn_tail_up(fc):
            seq, g, sc, slot, ts = fc["seq"], fc["g"], fc["sc"], fc["slot"], fc["ts"]
            ci = fc["ci"]
            row0 = fc["row0"]
            ffn_deps = []
            for half in (0, 1):
                stp = top(lambda e, half=half, ts=ts: e.matmul(
                    aux[:, half * 512: half * 512 + 512], hwt[slot % 2][:],
                    bcat[:, half * 512: half * 512 + 512],
                    start=False, stop=True),
                    waits=[fc["hwt_dep"], ("sw", w_bc)], inc="st")
                base_last[0] = stp
                evd = vop(lambda e, half=half, slot=slot: e.tensor_add(
                    x1b[slot][:, half * 512: half * 512 + 512],
                    x1b[slot][:, half * 512: half * 512 + 512],
                    aux[:, half * 512: half * 512 + 512]),
                    waits=[("st", stp), fc["nd2"]])
                ffn_deps.append(("sv", evd))
                setf(("aux", half), ("sv", evd))
            addf(("hwt", slot % 2), *ffn_deps)
            so_c = dma("sync", out_d[row0 + ci * 128: row0 + (ci + 1) * 128, :],
                       x1b[slot][:], waits=ffn_deps, sem="so")
            nstore[0] += 1
            setf(("x1", slot), ("so", so_c))

        boundary_nds = {}
        boundary_xl = {}
        for seq in range(NSEQ):
            row0 = seq * S
            ln1_done = [None] * NCH
            if seq > 0:
                w_kb = wk_load(waits=waitf(("qmix",)) + waitf(("ctx",)))

            kproj_last = 0
            xlv = {}

            def emit_loads(g):
                for c in range(4):
                    ci = g * 4 + c
                    slot = ci % 4
                    xlv[ci] = dma(
                        "sync", x1b[slot][:],
                        x_d[row0 + ci * 128: row0 + (ci + 1) * 128, :],
                        waits=waitf(("x1", slot)))

            def emit_lns(g):
                out = []
                for c in range(4):
                    ci = g * 4 + c
                    slot = ci % 4
                    nd = ln_norm(x1b[slot][:], xn1[:, ci * D:(ci + 1) * D],
                                 ci % 4, [("sx", xlv[ci])],
                                 dst_wait=waitf(("xn1",)))
                    ln1_done[ci] = nd
                    setf(("x1", slot), nd)
                    out.append(nd)
                return out

            if boundary_nds:
                nds = [boundary_nds[c] for c in range(4)]
                for c in range(4):
                    ln1_done[c] = nds[c]
                boundary_nds = {}
                boundary_xl = {}
            else:
                emit_loads(0)
                nds = emit_lns(0)
            for g in range(G):
                if g + 1 < G:
                    emit_loads(g + 1)
                td = []
                for c in range(4):
                    ci = g * 4 + c
                    tdp = dmat(
                        xnt.rearrange("p (k s) -> p k s", s=512)[:, :, c * 128:(c + 1) * 128],
                        xn1.rearrange("p (c k d) -> p c k d", c=NCH, k=KC)[:, ci],
                        waits=[nds[c]] + waitf(("xnt",)))
                    td.append(("sd", tdp))
                if g + 1 < G:
                    nds = emit_lns(g + 1)
                for m in range(KC):
                    sl = m % 4
                    pieces = [(wk_sl(k, m),
                               xnt[:, k * 512:(k + 1) * 512]) for k in range(KC)]
                    dst = kt[:, m * S + g * 512: m * S + (g + 1) * 512]
                    dep, stp = mm_group(
                        bigs(sl), pieces, waits=td + [("sw", w_kb)],
                        evac_fn=(lambda e, d=dst, sl=sl: e.copy(d, bigs(sl))),
                        psum_key=("big", sl))
                    kproj_last = stp
                setf(("xnt",), ("st", kproj_last))
            k_evac_ss = em.now("ss")
            setf(("qmix",), ("st", kproj_last))
            setf(("ctx",), ("st", kproj_last))

            def emit_qtp(g):
                td = []
                for c in range(4):
                    ci = g * 4 + c
                    tdp = dmat(
                        xnt.rearrange("p (k s) -> p k s", s=512)[:, :, c * 128:(c + 1) * 128],
                        xn1.rearrange("p (c k d) -> p c k d", c=NCH, k=KC)[:, ci],
                        waits=[ln1_done[ci]] + waitf(("xnt",)))
                    td.append(("sd", tdp))
                return td

            pending = []
            td_next = emit_qtp(0)
            for g in range(G):
                td = td_next
                q_deps = []
                q_last = 0
                for m in range(KC):
                    sl = m % 4
                    pieces = [(wsb_q[:, k * D + m * 128: k * D + (m + 1) * 128],
                               xnt[:, k * 512:(k + 1) * 512]) for k in range(KC)]
                    dst = qmix[:, m * 512:(m + 1) * 512]
                    dep, stp = mm_group(
                        bigs(sl), pieces,
                        waits=td + [("sw", w_q)] + waitf(("qmix",)),
                        evac_fn=(lambda e, d=dst, sl=sl: e.activation(
                            out=d, in_=bigs(sl), func=AF.Copy, scale=1.0 / 32.0)),
                        psum_key=("big", sl))
                    q_deps.append(dep)
                    q_last = stp
                free.pop(("qmix",), None)
                setf(("xnt",), ("st", q_last))
                if g + 1 < G:
                    td_next = emit_qtp(g + 1)
                pz = vop(lambda e, g=g: e.memset(
                    ptg[:, g * 4 * 512:(g * 4 + 4) * 512], 0.0),
                    waits=waitf(("ptg",)))
                free.pop(("ptg",), None)

                pt_sd = []
                exp_all = 0
                for qt in range(4):
                    qg = g * 4 + qt
                    tmax = (qg + 1) * 128
                    nt = ceil_div(tmax, 512)
                    if nt <= 2:
                        boff = 2 * (qt % 2)
                    elif nt == 3:
                        boff = 3 * (qt % 2)
                    else:
                        boff = 0
                    stps = []
                    pm_deps = []
                    if nt >= 3:
                        tc_order = [0, nt - 1] + list(range(1, nt - 1))
                    else:
                        tc_order = list(range(nt))
                    for tc in tc_order:
                        w = min(512, tmax - tc * 512)
                        stp = None
                        for k in range(KC):
                            ww = []
                            if k == 0:
                                ww = waitf(("big", (boff + tc) % 4))
                                if tc == 0:
                                    ww = ww + [("ss", k_evac_ss)] + \
                                        [(s, v) for (s, v) in q_deps]
                            stp = top(lambda e, k=k, tc=tc, w=w, qt=qt, boff=boff: e.matmul(
                                big[:, ((boff + tc) % 4) * 512: ((boff + tc) % 4) * 512 + w],
                                qmix[:, k * 512 + qt * 128: k * 512 + (qt + 1) * 128],
                                kt[:, k * S + tc * 512: k * S + tc * 512 + w],
                                start=(k == 0), stop=(k == KC - 1)),
                                waits=ww, inc="st" if k == KC - 1 else None)
                        stps.append(stp)
                        last_emitted = (tc == tc_order[-1])
                        if last_emitted and pending:
                            if qt >= 1:
                                ffn_tail_base(pending[qt - 1])
                                if qt + 1 < 4:
                                    ffn_head_dma(pending[qt + 1])
                        if tc == nt - 1:
                            off = qt * 128
                            mk = vop(lambda e, tc=tc, w=w, off=off, boff=boff: e.tensor_tensor(
                                big[:, ((boff + tc) % 4) * 512: ((boff + tc) % 4) * 512 + w],
                                big[:, ((boff + tc) % 4) * 512: ((boff + tc) % 4) * 512 + w],
                                mask[:, 384 - off: 384 - off + w], ALU.add),
                                waits=[("st", stp), ("sx", c_mask)])
                            pm = vop(lambda e, tc=tc, w=w, boff=boff: e.reduce_max(
                                pmx[:, tc:tc + 1],
                                big[:, ((boff + tc) % 4) * 512: ((boff + tc) % 4) * 512 + w],
                                axis=AX.X), waits=[("sv", mk)])
                        else:
                            pm = vop(lambda e, tc=tc, w=w, boff=boff: e.reduce_max(
                                pmx[:, tc:tc + 1],
                                big[:, ((boff + tc) % 4) * 512: ((boff + tc) % 4) * 512 + w],
                                axis=AX.X), waits=[("st", stp)] + waitf(("pmx",)))
                        pm_deps.append(("sv", pm))
                    nm_done = vop(lambda e, nt=nt, qt=qt: e.reduce_max(
                        nmx[:, qt:qt + 1], pmx[:, 0:nt], axis=AX.X, negate=True),
                        waits=pm_deps + waitf(("nmx", qt)))
                    free.pop(("nmx", qt), None)
                    setf(("pmx",), ("sv", nm_done))
                    e_done = 0
                    for tc in range(nt):
                        w = min(512, tmax - tc * 512)
                        e_done = sop(lambda e, tc=tc, w=w, qt=qt, boff=boff: e.activation(
                            out=pbf[:, tc * 512: tc * 512 + w],
                            in_=big[:, ((boff + tc) % 4) * 512: ((boff + tc) % 4) * 512 + w],
                            func=AF.Exp, bias=nmx[:, qt:qt + 1],
                            accum_out=psum_s[:, tc:tc + 1]),
                            waits=[("sv", nm_done)] + waitf(("pbf",)) +
                                  waitf(("psum_s",)))
                        setf(("big", (boff + tc) % 4), ("ss", e_done))
                    free.pop(("pbf",), None)
                    free.pop(("psum_s",), None)
                    exp_all = e_done
                    setf(("nmx", qt), ("ss", e_done))
                    if pending and qt >= 1:
                        ffn_hwt(pending[qt - 1])
                    rs = vop(lambda e, nt=nt: e.reduce_sum(
                        rsum[:], psum_s[:, 0:nt], axis=AX.X),
                        waits=[("ss", e_done)])
                    setf(("psum_s",), ("sv", rs))
                    rc = vop(lambda e, qt=qt: e.reciprocal(
                        recg[:, qt:qt + 1], rsum[:]),
                        waits=[("sv", rs)] + waitf(("recg", qt)))
                    free.pop(("recg", qt), None)
                    setf(("recg2", qt), ("sv", rc))
                    ntc2 = qg + 1
                    pt = dmat(
                        ptg.rearrange("p (c q) -> p c q", q=512)[:, 0:ntc2, qt * 128:(qt + 1) * 128],
                        pbf.rearrange("p (c t) -> p c t", t=128)[:, 0:ntc2, :],
                        waits=[("ss", e_done), ("sv", pz)] + waitf(("ptg_w",)))
                    pt_sd.append(pt)
                    setf(("pbf",), ("sd", pt))
                    if pending:
                        if qt >= 1:
                            ffn_tail_up(pending[qt - 1])
                        ffn_head(pending[qt])
                if pending:
                    ffn_tail_base(pending[3])
                    ffn_hwt(pending[3])
                    ffn_tail_up(pending[3])
                    pending = []

                xlb = {}
                for sc in range(4):
                    ci = g * 4 + sc
                    xlb[sc] = dma("sync", x1b[sc][:],
                                  x_d[row0 + ci * 128: row0 + (ci + 1) * 128, :],
                                  waits=waitf(("x1", sc)))
                ntc2 = (g + 1) * 4
                mix_deps = []
                mix_last = 0
                for dch in range(KC):
                    sl = dch % 4
                    pieces = [(xn1[:, tc2 * D + dch * 128: tc2 * D + (dch + 1) * 128],
                               ptg[:, tc2 * 512:(tc2 + 1) * 512])
                              for tc2 in range(ntc2)]
                    dst = qmix[:, dch * 512:(dch + 1) * 512]
                    dep, stp = mm_group(
                        bigs(sl), pieces,
                        waits=[("sd", pt_sd[-1]), ("ss", exp_all)],
                        evac_fn=(lambda e, d=dst, sl=sl: e.copy(d, bigs(sl))),
                        psum_key=("big", sl))
                    mix_deps.append(dep)
                    mix_last = stp
                setf(("ptg_w",), ("st", mix_last))
                setf(("ptg",), ("st", mix_last))
                ctx_deps = []
                ctx_last = 0
                for dv in range(KC):
                    sl = dv % 4
                    pieces = [(wsb_v[:, k * D + dv * 128: k * D + (dv + 1) * 128],
                               qmix[:, k * 512:(k + 1) * 512]) for k in range(KC)]
                    dst = ctx[:, dv * 512:(dv + 1) * 512]
                    dep, stp = mm_group(
                        bigs(sl), pieces,
                        waits=[(s, v) for (s, v) in mix_deps] + [("sw", w_v)] +
                              waitf(("ctx",)),
                        evac_fn=(lambda e, d=dst, sl=sl: e.copy(d, bigs(sl))),
                        psum_key=("big", sl))
                    ctx_deps.append(dep)
                    ctx_last = stp
                free.pop(("ctx",), None)
                setf(("qmix",), ("st", ctx_last))

                wo_last = 0
                for sc in range(4):
                    ci = g * 4 + sc
                    slot = sc
                    xl = xlb[sc]
                    at_deps = []
                    for half in (0, 1):
                        sl = (sc % 2) * 2 + half
                        pieces = [(ctx[:, dv * 512 + sc * 128: dv * 512 + (sc + 1) * 128],
                                   wsb_o[:, dv * D + half * 512: dv * D + half * 512 + 512])
                                  for dv in range(KC)]
                        def ev(e, half=half, sl=sl, slot=slot, sc=sc):
                            return e.scalar_tensor_tensor(
                                out=x1b[slot][:, half * 512: half * 512 + 512],
                                in0=bigs(sl), scalar=recg[:, sc:sc + 1],
                                in1=x1b[slot][:, half * 512: half * 512 + 512],
                                op0=ALU.mult, op1=ALU.add)
                        dep, stp = mm_group(
                            bigs(sl), pieces,
                            waits=[(s, v) for (s, v) in ctx_deps] +
                                  [("sw", w_o), ("sx", xl)] +
                                  waitf(("recg2", sc)),
                            evac_fn=ev, evac_eng="vector", psum_key=("big", sl))
                        at_deps.append(dep)
                        wo_last = stp
                    setf(("recg", sc), *at_deps)
                    nd2 = ln_norm(x1b[slot][:], xn2[sc][:], sc % 4,
                                  [(s, v) for (s, v) in at_deps],
                                  dst_wait=waitf(("xn2", sc)))
                    pending.append(dict(seq=seq, g=g, sc=sc, ci=ci, slot=slot,
                                        nd2=nd2, row0=row0, w_base=w_base))
                    if sc <= 1:
                        ffn_head_dma(pending[-1])

                setf(("ctx",), ("st", wo_last))
                addf(("xn1",), ("st", mix_last))
                if g == G - 1:
                    ffn_head(pending[0])
                    ffn_head(pending[1])
                    ffn_tail_base(pending[0])
                    ffn_hwt(pending[0])
                    ffn_tail_up(pending[0])
                    ffn_head_dma(pending[2])
                    ffn_head(pending[2])
                    ffn_tail_base(pending[1])
                    ffn_hwt(pending[1])
                    ffn_tail_up(pending[1])
                    ffn_head_dma(pending[3])
                    ffn_head(pending[3])
                    ffn_tail_base(pending[2])
                    ffn_hwt(pending[2])
                    ffn_tail_up(pending[2])
                    ffn_tail_base(pending[3])
                    ffn_hwt(pending[3])
                    ffn_tail_up(pending[3])

            nrow0 = (seq + 1) * S

            def bnd(c):
                if seq + 1 >= NSEQ:
                    return
                boundary_xl[c] = dma(
                    "sync", x1b[c][:],
                    x_d[nrow0 + c * 128: nrow0 + (c + 1) * 128, :],
                    waits=waitf(("x1", c)))
                nd = ln_norm(x1b[c][:], xn1[:, c * D:(c + 1) * D], c % 4,
                             [("sx", boundary_xl[c])],
                             dst_wait=waitf(("xn1",)))
                boundary_nds[c] = nd
                setf(("x1", c), nd)

            bnd(0)
            bnd(1)
            bnd(2)
            bnd(3)
            pending = []

        em.op("sync", lambda e: None, waits=[("so", em.now("so"))])

        block = st_.enter_context(nc.Block())

        def runner(qname):
            def run(eng):
                for ws, fn, inc, ib in em.q[qname]:
                    for s, v in ws:
                        eng.wait_ge(sems[s], v)
                    ins = fn(eng)
                    if inc is not None and ins is not None:
                        ins.then_inc(sems[inc], ib)
            return run

        block.tensor(runner("tensor"))
        block.vector(runner("vector"))
        block.scalar(runner("scalar"))
        block.sync(runner("sync"))
        block.gpsimd(runner("gpsimd"))

    return nc


def _host_layout(inputs, S, NSEQ):
    x = np.ascontiguousarray(np.asarray(inputs["L0_attn_x"], dtype=np.float32))
    B = x.shape[0]
    ncore = B // NSEQ
    wq = np.ascontiguousarray(np.asarray(inputs["L0_attn_Wq"], np.float32))
    wk = np.ascontiguousarray(np.asarray(inputs["L0_attn_Wk"], np.float32))
    wv = np.ascontiguousarray(np.asarray(inputs["L0_attn_Wv"], np.float32))
    wo = np.ascontiguousarray(np.asarray(inputs["L0_attn_Wo"], np.float32))
    wbase = np.ascontiguousarray(np.asarray(inputs["L0_ffn_base_W"], np.float32).T)
    shared_A = np.asarray(inputs["L0_ffn_shared_A"], np.float32)
    shared_B = np.asarray(inputs["L0_ffn_shared_B"], np.float32)
    routed_A = np.asarray(inputs["L0_ffn_routed_A"], np.float32)
    routed_B = np.asarray(inputs["L0_ffn_routed_B"], np.float32)
    router_W = np.asarray(inputs["L0_ffn_router_W"], np.float32)
    router_b = np.asarray(inputs["L0_ffn_router_b"], np.float32)
    base_b = np.asarray(inputs["L0_ffn_base_b"], np.float32)
    E = routed_A.shape[0]

    acat = np.zeros((D, 120), np.float32)
    acat[:, 0:16] = shared_A.T
    for e in range(E):
        acat[:, 16 + 16 * e: 32 + 16 * e] = routed_A[e].T
    acat[:, 112:118] = router_W.T
    acat = np.ascontiguousarray(acat)

    bcat = np.zeros((128, D), np.float32)
    bcat[0:16, :] = shared_B.T
    for e in range(E):
        bcat[16 + 16 * e: 32 + 16 * e, :] = routed_B[e].T
    bcat[112, :] = base_b
    bcat = np.ascontiguousarray(bcat)

    rb = np.zeros((128, 8), np.float32)
    rb[:, 0:6] = router_b[None, :]

    mask = np.zeros((128, 512), np.float32)
    ss, cc = np.meshgrid(np.arange(128), np.arange(512), indexing="ij")
    mask[cc > ss + 384] = MASKV
    mask = mask.astype(BF)

    ident = np.eye(128, dtype=BF)
    shared = {"wq": wq, "wk": wk, "wv": wv, "wo": wo, "wbase": wbase,
              "acat": acat, "bcat": bcat, "rb": rb, "mask": mask,
              "ident": ident}
    in_maps = []
    for c in range(ncore):
        m = dict(shared)
        m["x"] = np.ascontiguousarray(
            x[c * NSEQ:(c + 1) * NSEQ].reshape(NSEQ * S, D))
        in_maps.append(m)
    return in_maps, ncore


def kernel(**inputs):
    S = 2048
    NSEQ = 2
    in_maps, ncore = _host_layout(inputs, S, NSEQ)
    nc = build_program(S, NSEQ)
    res = run_bass_kernel_spmd(nc, in_maps, core_ids=list(range(ncore)))
    outs = [np.asarray(res.results[c]["out"]).reshape(NSEQ, S, D)
            for c in range(ncore)]
    return np.concatenate(outs, axis=0)

